# version 01 (final)
# baseline (speedup 1.0000x reference)
import numpy as np
from contextlib import ExitStack

import concourse.bass as bass
import concourse.mybir as mybir
from concourse.bass_utils import run_bass_kernel_spmd
from concourse.alu_op_type import AluOpType as OP

F32 = mybir.dt.float32
BF16 = mybir.dt.bfloat16
I32 = mybir.dt.int32
U8 = mybir.dt.uint8
AF = mybir.ActivationFunctionType

T = 2048
D = 1024
NJ = 8
TC = 512
NCH = T // TC
DBG_STOP = ""
DBG_MT = None



class Prog:

    ENGS = ("t", "v", "s", "g", "y")
    SEMS = ("t", "v", "s", "g", "y", "dg", "dy")

    def __init__(self, nc):
        self.nc = nc
        self.stream = {e: [] for e in self.ENGS}
        self.count = {c: 0 for c in self.SEMS}
        self.sem = {}

    def op(self, e, fn, inc=0, raw=False):
        if e in ("v", "s", "g") and inc != 16 and not raw:
            if self.count[e] > 0:
                self.wait(e, e, self.count[e])
            inc = max(inc, 1)
        if e == "t" and inc != 16 and not raw:
            inc = max(inc, 1)
        ch = ("d" + e) if inc == 16 else e
        if inc:
            self.count[ch] += inc
            sem_map = self.sem

            def wrapped(eng, fn=fn, ch=ch, inc=inc):
                fn(eng).then_inc(sem_map[ch], inc)
            self.stream[e].append(wrapped)
        else:
            self.stream[e].append(lambda eng, fn=fn: fn(eng))
        return (ch, self.count[ch])

    def wait(self, e, src, val=None):
        v = self.count[src] if val is None else val
        if v <= 0:
            return
        sem_map = self.sem
        self.stream[e].append(lambda eng, s=src, v=v: eng.wait_ge(sem_map[s], v))

    def tok(self, e):
        return (e, self.count[e])

    def wait_tok(self, e, tok):
        if tok is not None:
            self.wait(e, tok[0], tok[1])

    def barrier(self):
        snap = dict(self.count)
        for e in self.ENGS:
            for o in self.SEMS:
                if o != e:
                    self.wait(e, o, snap[o])

    def replay(self, block):
        secs = {"t": block.tensor, "v": block.vector, "s": block.scalar,
                "g": block.gpsimd, "y": block.sync}
        for e in self.ENGS:
            ops = self.stream[e]

            def body(eng, ops=ops):
                for f in ops:
                    f(eng)
            secs[e](body)


class Ctx:
    def __init__(self, nc):
        self.nc = nc
        self.pr = Prog(nc)
        self.sb = {}
        self.dram = {}
        self.ps = {}
        self.din = {}
        self.pbtok = {0: None, 1: None, 2: None, "bt": None}

    def alloc(self, stack, name, shape, dt):
        h = stack.enter_context(self.nc.sbuf_tensor("sb_" + name, shape, dt))
        self.sb[name] = h
        return h


def bankA(ctx, i):
    return ctx.ps["PA"][:, i * TC:(i + 1) * TC]


def bankB(ctx, i):
    return ctx.ps["PB"][:, i * TC:(i + 1) * TC]



def emit_glue_ln(ctx, want_pooled=False, dst="XN", gam=None, bet=None, src="RX", stop_after=99, src_bf16=False):
    pr, sb = ctx.pr, ctx.sb
    RX, XN = sb[src], sb[dst]
    PA = ctx.ps["PA"]
    RSTB, MRB = sb["RSTB"], sb["MRB"]
    XT0, XT1 = sb["CT0"], sb["CT1"]
    ones_b, ones_f = sb["ONESB"], sb["ONESF1"]
    eps_t, negh = sb["EPS"], sb["NEGH"]

    SR = sb["STATR"]
    MU = SR[0:1, 0:T]
    M2 = SR[0:1, T:2 * T]
    RST = SR[0:1, T:2 * T]
    LNV = SR[0:1, 2 * T:3 * T]
    MR = SR[0:1, 2 * T:3 * T]

    buf_guard = [None, None]
    for rnd, SQ in ((0, False), (1, True)):
        for j in range(NJ):
            buf = (XT0, XT1)[j % 2]
            if src_bf16 and not SQ:
                buf = RX[:, j * T:(j + 1) * T]
            else:
                pr.wait_tok("v", buf_guard[j % 2])
                if SQ:
                    tk = pr.op("v", lambda v, j=j, buf=buf: v.tensor_tensor(
                        buf[:, :], RX[:, j * T:(j + 1) * T],
                        RX[:, j * T:(j + 1) * T], op=OP.mult), inc=1)
                else:
                    tk = pr.op("v", lambda v, j=j, buf=buf: v.tensor_copy(
                        buf[:, :], RX[:, j * T:(j + 1) * T]), inc=1)
                pr.wait_tok("t", tk)
            for c in range(NCH):
                pr.op("t", lambda t, c=c, j=j, buf=buf: t.matmul(
                    PA[0:1, c * TC:(c + 1) * TC], ones_b[:, :],
                    buf[:, c * TC:(c + 1) * TC],
                    start=(j == 0), stop=(j == NJ - 1)),
                    inc=(1 if c == NCH - 1 else 0))
            buf_guard[j % 2] = pr.tok("t")
        mm_tok = pr.tok("t")
        pr.wait_tok("v", mm_tok)
        row = MU if not SQ else M2
        pr.op("v", lambda v, row=row: v.tensor_single_scalar(
            row, PA[0:1, 0:T], 1.0 / D, op=OP.mult), inc=1)
        pr.wait_tok("t", pr.tok("v"))

    if stop_after <= 1:
        return pr.tok("v")
    pr.op("v", lambda v: v.tensor_tensor(LNV, MU, MU, op=OP.mult), inc=1)
    pr.op("v", lambda v: v.tensor_tensor(M2, M2, LNV, op=OP.subtract), inc=1)
    pr.wait_tok("s", pr.tok("v"))
    pr.op("s", lambda s: s.activation(LNV, M2, AF.Ln, bias=eps_t[0:1, 0:1]), inc=1)
    pr.op("s", lambda s: s.activation(RST, LNV, AF.Exp, scale=negh[0:1, 0:1]), inc=1)
    pr.op("s", lambda s: s.drain(), raw=True)
    pr.wait_tok("v", pr.tok("s"))
    pr.op("v", lambda v: v.tensor_tensor(MR, MU, RST, op=OP.mult), inc=1)

    if stop_after <= 2:
        return pr.tok("v")
    for rnd, (row, dstb) in enumerate(((RST, RSTB), (MR, MRB))):
        pr.wait_tok("t", pr.tok("v"))
        for c in range(NCH):
            pr.op("t", lambda t, c=c, row=row: t.matmul(
                PA[:, c * TC:(c + 1) * TC], ones_f[:, :],
                row[0:1, c * TC:(c + 1) * TC], start=True, stop=True),
                inc=(1 if c == NCH - 1 else 0))
        pr.wait_tok("v", pr.tok("t"))
        for c in range(NCH):
            pr.op("v", lambda v, dstb=dstb, c=c: v.tensor_copy(
                dstb[:, c * TC:(c + 1) * TC], PA[:, c * TC:(c + 1) * TC]), inc=1)

    if stop_after <= 3:
        return pr.tok("v")
    TMP = sb["CT0"]
    for j in range(NJ):
        pr.op("v", lambda v, j=j: v.tensor_tensor(
            TMP[:, :], RX[:, j * T:(j + 1) * T], RSTB[:, :], op=OP.mult), inc=1)
        if gam is None:
            if want_pooled:
                pr.op("v", lambda v, j=j: v.scalar_tensor_tensor(
                    XN[:, j * T:(j + 1) * T], TMP[:, :], 0.0, MRB[:, :],
                    op0=OP.add, op1=OP.subtract, accum_out=sb["POOL"][:, j:j + 1]), inc=1)
            else:
                pr.op("v", lambda v, j=j: v.tensor_tensor(
                    XN[:, j * T:(j + 1) * T], TMP[:, :], MRB[:, :], op=OP.subtract), inc=1)
        else:
            G, B = sb[gam], sb[bet]
            pr.op("v", lambda v, j=j: v.tensor_tensor(
                TMP[:, :], TMP[:, :], MRB[:, :], op=OP.subtract), inc=1)
            pr.op("v", lambda v, j=j, G=G, B=B: v.tensor_scalar(
                XN[:, j * T:(j + 1) * T], TMP[:, :], G[:, j:j + 1], B[:, j:j + 1],
                op0=OP.mult, op1=OP.add), inc=1)
    return pr.tok("v")



def emit_wstat(ctx, wdram, rhs, evac, n_mt=8, n_kt=8, wcols=1024, banks=(0, 1, 2),
               rhs_stride=T):
    pr, sb = ctx.pr, ctx.sb
    PB = ctx.ps["PB"]
    wbufs = (sb["CT0"], sb["CT1"])
    pr.wait("g", "t")
    pr.wait("g", "v")
    pr.wait("g", "s")
    wfree = [None, None]
    bank_tok = ctx.pbtok
    for mt in range(n_mt):
        wb = wbufs[mt % 2]
        pr.wait_tok("g", wfree[mt % 2])
        src = wdram.rearrange("(kt p) o -> p kt o", p=128)[:, :, mt * 128:(mt + 1) * 128]
        wtok = pr.op("g", lambda g, wb=wb, src=src: g.dma_start(
            out=wb[:, 0:n_kt * 128].rearrange("p (kt o) -> p kt o", o=128),
            in_=src), inc=16)
        pr.wait_tok("t", wtok)
        for tc in range(NCH):
            b = banks[(mt * NCH + tc) % len(banks)]
            pr.wait_tok("t", bank_tok[b])
            bank = PB[:, b * TC:(b + 1) * TC]
            for kt in range(n_kt):
                pr.op("t", lambda t, bank=bank, wb=wb, kt=kt, tc=tc: t.matmul(
                    bank, wb[:, kt * 128:(kt + 1) * 128],
                    rhs[:, kt * rhs_stride + tc * TC: kt * rhs_stride + (tc + 1) * TC],
                    start=(kt == 0), stop=(kt == n_kt - 1)),
                    inc=(1 if kt == n_kt - 1 else 0))
            bank_tok[b] = evac(mt, tc, bank)
        wfree[mt % 2] = pr.tok("t")



def emit_l0_attn(ctx):
    pr, sb = ctx.pr, ctx.sb
    XN, RX = sb["XN"], sb["RX"]
    A, B = sb["ARA"], sb["ARB"]
    PA = ctx.ps["PA"]
    SM = sb["SMALLS"]
    LAM = SM[:, 0:NJ]

    pr.wait_tok("s", pr.tok("v"))
    pr.op("s", lambda s: s.activation(LAM, sb["B128s"][:, :], AF.Exp,
                                      scale=sb["NEGLN2"][:, :]), inc=1)
    pr.op("s", lambda s: s.drain(), raw=True)
    pr.wait_tok("v", pr.tok("s"))
    pr.op("v", lambda v: v.tensor_scalar(LAM, LAM, -1.0, 1.0,
                                         op0=OP.mult, op1=OP.add), inc=1)
    pr.op("v", lambda v: v.tensor_scalar(LAM, LAM, 1.1754943508222875e-38,
                                         1.0 - 1e-9, op0=OP.max, op1=OP.min), inc=1)
    lam_tok = pr.tok("v")

    def ev_copy_B(mt, tc, bank):
        pr.wait_tok("v", pr.tok("t"))
        return pr.op("v", lambda v, mt=mt, tc=tc, bank=bank: v.tensor_copy(
            B[:, mt * T + tc * TC: mt * T + (tc + 1) * TC], bank), inc=1)

    emit_wstat(ctx, ctx.din["L0WVT"], XN[:, :], ev_copy_B)

    pr.wait_tok("v", lam_tok)
    for j in range(NJ):
        pr.wait_tok("v", pr.tok("t"))
        pr.op("v", lambda v, j=j: v.memset(PA[:, 0:T], 1.0), inc=1)
        pr.op("v", lambda v, j=j: v.tensor_scalar(
            PA[:, 0:T], PA[:, 0:T], LAM[:, j:j + 1], None, op0=OP.mult), inc=1)
        pr.op("v", lambda v, j=j: v.tensor_tensor_scan(
            B[:, j * T:(j + 1) * T], PA[:, 0:T], B[:, j * T:(j + 1) * T], 0.0,
            op0=OP.mult, op1=OP.add), inc=1)
    scan_tok = pr.tok("v")

    def ev_copy_A(mt, tc, bank):
        pr.wait_tok("v", pr.tok("t"))
        return pr.op("v", lambda v, mt=mt, tc=tc, bank=bank: v.tensor_copy(
            A[:, mt * T + tc * TC: mt * T + (tc + 1) * TC], bank), inc=1)

    emit_wstat(ctx, ctx.din["L0WQT"], XN[:, :], ev_copy_A)

    pr.wait_tok("v", scan_tok)
    for j in range(NJ):
        pr.op("v", lambda v, j=j: v.tensor_tensor(
            A[:, j * T:(j + 1) * T], A[:, j * T:(j + 1) * T],
            B[:, j * T:(j + 1) * T], op=OP.mult), inc=1)
    ret_tok = pr.tok("v")

    def ev_silu_B(mt, tc, bank):
        pr.wait_tok("s", pr.tok("t"))
        return pr.op("s", lambda s, mt=mt, tc=tc, bank=bank: s.activation(
            B[:, mt * T + tc * TC: mt * T + (tc + 1) * TC], bank, AF.Silu), inc=1)

    emit_wstat(ctx, ctx.din["L0WGT"], XN[:, :], ev_silu_B)
    g_tok = pr.tok("s")

    pr.wait_tok("t", ret_tok)
    emit_glue_ln(ctx, src="ARA", dst="XN", gam="L0G", bet="L0B", src_bf16=True)

    pr.wait_tok("v", g_tok)
    for j in range(NJ):
        pr.op("v", lambda v, j=j: v.tensor_tensor(
            B[:, j * T:(j + 1) * T], XN[:, j * T:(j + 1) * T],
            B[:, j * T:(j + 1) * T], op=OP.mult), inc=1)
    yg_tok = pr.tok("v")
    pr.wait_tok("t", yg_tok)

    def ev_res(mt, tc, bank):
        pr.wait_tok("v", pr.tok("t"))
        sl = slice(mt * T + tc * TC, mt * T + (tc + 1) * TC)
        return pr.op("v", lambda v, bank=bank, sl=sl: v.scalar_tensor_tensor(
            RX[:, sl], bank, 1.0, RX[:, sl], op0=OP.mult, op1=OP.add), inc=1)

    emit_wstat(ctx, ctx.din["L0WOT"], B[:, :], ev_res)



def emit_l1_ffn(ctx, coef=0.5):
    pr, sb = ctx.pr, ctx.sb
    RX, XN, POOL = sb["RX"], sb["XN"], sb["POOL"]
    PB = ctx.ps["PB"]
    GWT, GBr = sb["GWT"], sb["L1GB"]
    LGROW = sb["LGROW"]
    U8S = sb["U8S"]
    NEGB = sb["NEGB8"]
    GAT128 = sb["GAT128"]
    EB = sb["L1EB"]
    GSC = sb["GSC"]
    ones_f = sb["ONESF1"]

    ld0 = pr.op("y", lambda y: y.dma_start(out=GBr[0:1, :],
                                           in_=ctx.din["L1GBr"][:, :]), inc=16)
    pr.wait_tok("v", ld0)
    pr.op("v", lambda v: v.memset(NEGB[0:1, :], -3.0e38), inc=1)
    pr.op("v", lambda v: v.drain(), raw=True)
    LG = LGROW[0:1, 0:8]
    M1 = LGROW[0:1, 8:9]
    M2 = LGROW[0:1, 9:10]
    NMX = LGROW[0:1, 10:11]
    SS = LGROW[0:1, 11:12]
    RS = LGROW[0:1, 12:13]
    L2r = LGROW[0:1, 16:24]
    LMr = LGROW[0:1, 24:32]
    ELr = LGROW[0:1, 32:40]
    GTS = LGROW[0:1, 40:48]

    pr.wait_tok("t", pr.tok("v"))
    b = 0
    pr.wait_tok("t", ctx.pbtok[b])
    bank = PB[:, b * TC:(b + 1) * TC]
    for j in range(NJ):
        pr.op("t", lambda t, j=j, bank=bank: t.matmul(
            bank[0:1, 0:8], POOL[:, j:j + 1], GWT[:, 8 * j:8 * j + 8],
            start=(j == 0), stop=(j == NJ - 1)), inc=(1 if j == NJ - 1 else 0))
    pr.wait_tok("v", pr.tok("t"))
    pr.op("v", lambda v, bank=bank: v.tensor_single_scalar(
        LG, bank[0:1, 0:8], 1.0 / T, op=OP.mult), inc=1)
    ctx.pbtok[b] = pr.tok("v")
    pr.op("v", lambda v: v.tensor_tensor(LG, LG, GBr[0:1, :], op=OP.add), inc=1)
    pr.op("v", lambda v: v.reduce_max(M1, LG, axis=mybir.AxisListType.X), inc=1)
    pr.op("v", lambda v: v.tensor_scalar(U8S[0:1, 0:8], LG, M1, None, op0=OP.is_equal), inc=1)
    pr.op("v", lambda v: v.select(L2r, U8S[0:1, 0:8], NEGB[0:1, :], LG), inc=1)
    pr.op("v", lambda v: v.reduce_max(M2, L2r, axis=mybir.AxisListType.X), inc=1)
    pr.op("v", lambda v: v.tensor_scalar(U8S[0:1, 0:8], LG, M2, None, op0=OP.is_lt), inc=1)
    pr.op("v", lambda v: v.select(LMr, U8S[0:1, 0:8], NEGB[0:1, :], LG), inc=1)
    pr.op("v", lambda v: v.reduce_max(M1, LMr, axis=mybir.AxisListType.X), inc=1)
    pr.op("v", lambda v: v.tensor_single_scalar(NMX, M1, -1.0, op=OP.mult), inc=1)
    pr.wait_tok("s", pr.tok("v"))
    pr.op("s", lambda s: s.activation(ELr, LMr, AF.Exp, bias=NMX), inc=1)
    pr.op("s", lambda s: s.drain(), raw=True)
    pr.wait_tok("v", pr.tok("s"))
    pr.op("v", lambda v: v.reduce_sum(SS, ELr, axis=mybir.AxisListType.X), inc=1)
    pr.op("v", lambda v: v.reciprocal(RS, SS), inc=1)
    pr.op("v", lambda v, coef=coef: v.tensor_scalar(
        GTS, ELr, RS, coef, op0=OP.mult, op1=OP.mult), inc=1)
    b = 1
    pr.wait_tok("t", pr.tok("v"))
    pr.wait_tok("t", ctx.pbtok[b])
    bank = PB[:, b * TC:(b + 1) * TC]
    pr.op("t", lambda t, bank=bank: t.matmul(
        bank[:, 0:8], ones_f[:, :], GTS, start=True, stop=True), inc=1)
    pr.wait_tok("v", pr.tok("t"))
    pr.op("v", lambda v, bank=bank: v.tensor_copy(GAT128[:, :], bank[:, 0:8]), inc=1)
    ctx.pbtok[b] = pr.tok("v")

    ew = ctx.din["L1EW"].rearrange("(e r) o -> e r o", e=8)
    gsc_guard = [None, None]
    for e in range(8):
        def ev(mt, tc, bank, e=e):
            slot = (mt * NCH + tc) % 2
            pr.wait_tok("s", pr.tok("t"))
            pr.wait_tok("s", gsc_guard[slot])
            pr.op("s", lambda s, mt=mt, bank=bank, slot=slot, e=e: s.activation(
                GSC[:, slot * TC:(slot + 1) * TC], bank, AF.Gelu,
                bias=EB[:, 8 * e + mt:8 * e + mt + 1]), inc=1)
            pr.wait_tok("v", pr.tok("s"))
            sl = slice(mt * T + tc * TC, mt * T + (tc + 1) * TC)
            tok = pr.op("v", lambda v, sl=sl, slot=slot, e=e: v.scalar_tensor_tensor(
                RX[:, sl], GSC[:, slot * TC:(slot + 1) * TC], GAT128[:, e:e + 1],
                RX[:, sl], op0=OP.mult, op1=OP.add), inc=1)
            gsc_guard[slot] = tok
            return pr.tok("s")
        emit_wstat(ctx, ew[e], XN[:, :], ev)



def emit_l1_attn(ctx):
    pr, sb = ctx.pr, ctx.sb
    RX, XN = sb["RX"], sb["XN"]
    A, B = sb["ARA"], sb["ARB"]
    PA, PB = ctx.ps["PA"], ctx.ps["PB"]
    kd, vd, pd = ctx.dram["L1KD"], ctx.dram["L1VD"], ctx.dram["L1PD"]
    KVST = sb["KVST"]
    TROW = sb["TROW"]
    TMP128 = sb["TMP128"]
    ZZ = sb["ZROW"]
    QLB = sb["QLB"]
    N20 = sb["NEG20"]
    ones_f = sb["ONESF1"]

    ldr = []
    for dstn, srcn in [("L1BKr", "L1BKr"), ("L1BVr", "L1BVr"), ("TROW", "L1TROW")]:
        ldr.append(pr.op("y", lambda y, dstn=dstn, srcn=srcn: y.dma_start(
            out=sb[dstn][0:1, :], in_=ctx.din[srcn][:, :]), inc=16))
    pr.wait_tok("t", ldr[-1])
    b = 0
    pr.wait_tok("t", ctx.pbtok[b])
    pr.wait_tok("t", pr.tok("v"))
    bank = PB[:, b * TC:(b + 1) * TC]
    pr.op("t", lambda t, bank=bank: t.matmul(
        bank[:, 0:16], ones_f[:, :], TROW[0:1, :], start=True, stop=True), inc=1)
    pr.wait_tok("v", pr.tok("t"))
    pr.op("v", lambda v, bank=bank: v.tensor_copy(TMP128[:, :], bank[:, 0:16]), inc=1)
    ctx.pbtok[b] = pr.tok("v")

    for wname, brow, dram in (("L1WK", "L1BKr", kd), ("L1WV", "L1BVr", vd)):
        pr.wait("g", "t")
        pr.wait("g", "v")
        wtok = pr.op("g", lambda g, wname=wname: g.dma_start(
            out=B[:, 0:8192].rearrange("p (kt o) -> p kt o", o=1024),
            in_=ctx.din[wname].rearrange("(kt p) o -> p kt o", p=128)), inc=16)
        bb = 1
        pr.wait_tok("t", ctx.pbtok[bb])
        bank = PB[:, bb * TC:(bb + 1) * TC]
        for oc in range(2):
            pr.op("t", lambda t, bank=bank, oc=oc, brow=brow: t.matmul(
                bank if oc == 0 else PB[:, 2 * TC:3 * TC],
                ones_f[:, :], sb[brow][0:1, oc * TC:(oc + 1) * TC],
                start=True, stop=True), inc=1)
        pr.wait_tok("v", pr.tok("t"))
        pr.op("v", lambda v, bank=bank: v.tensor_copy(sb["BCB"][:, 0:TC], bank), inc=1)
        pr.op("v", lambda v: v.tensor_copy(sb["BCB"][:, TC:2 * TC], PB[:, 2 * TC:3 * TC]), inc=1)
        ctx.pbtok[1] = pr.tok("v")
        ctx.pbtok[2] = pr.tok("v")
        pr.wait_tok("t", wtok)
        kst_guard = [None, None]
        for h in range(16):
            slot = h % 2
            for oc in range(2):
                bb = 1 + (2 * h + oc) % 2
                pr.wait_tok("t", ctx.pbtok[bb])
                bank = PB[:, bb * TC:(bb + 1) * TC]
                for j in range(NJ):
                    pr.op("t", lambda t, bank=bank, j=j, h=h, oc=oc: t.matmul(
                        bank, XN[:, j * T + h * 128: j * T + (h + 1) * 128],
                        B[:, j * 1024 + oc * TC: j * 1024 + (oc + 1) * TC],
                        start=(j == 0), stop=(j == NJ - 1)),
                        inc=(1 if j == NJ - 1 else 0))
                pr.wait_tok("v", pr.tok("t"))
                pr.wait_tok("v", kst_guard[slot])
                ctx.pbtok[bb] = pr.op("v", lambda v, bank=bank, oc=oc, slot=slot: v.tensor_tensor(
                    KVST[:, slot * 1024 + oc * TC: slot * 1024 + (oc + 1) * TC],
                    bank, sb["BCB"][:, oc * TC:(oc + 1) * TC], op=OP.add), inc=1)
            ev = pr.tok("v")
            pr.wait_tok("y", ev)
            kst_guard[slot] = pr.op("y", lambda y, dram=dram, h=h, slot=slot: y.dma_start(
                out=dram.rearrange("(h p f) -> h p f", h=16, p=128)[h],
                in_=KVST[:, slot * 1024:(slot + 1) * 1024]), inc=16)

    def ev_q(mt, tc, bank):
        pr.wait_tok("v", pr.tok("t"))
        return pr.op("v", lambda v, mt=mt, tc=tc, bank=bank: v.tensor_scalar(
            A[:, mt * T + tc * TC: mt * T + (tc + 1) * TC], bank,
            sb["L1BQ"][:, mt:mt + 1], None, op0=OP.add), inc=1)

    emit_wstat(ctx, ctx.din["L1WQ"], XN[:, :], ev_q)

    pr.wait("y", "v")
    pr.wait("y", "dy")
    for pair in range(8):
        rb = []
        for half, hh in ((0, 2 * pair), (1, 2 * pair + 1)):
            for dram, base in ((kd, 0), (vd, 2048)):
                rb.append(pr.op("y", lambda y, dram=dram, hh=hh, half=half, base=base: y.dma_start(
                    out=B[64 * half:64 * half + 64, base:base + 2048],
                    in_=dram.rearrange("(h dh f) -> h dh f", h=16, dh=64)[hh]), inc=16))
        pr.wait_tok("t", rb[-1])
        for half in range(2):
            hh = 2 * pair + half
            for tt in range(16):
                par, omt = tt // 8, tt % 8
                for c in range(3):
                    pr.wait_tok("t", ctx.pbtok[c])
                    pr.op("t", lambda t, c=c, half=half, tt=tt: t.matmul(
                        PB[:, c * TC:(c + 1) * TC],
                        B[64 * half:64 * half + 64, tt * 128:(tt + 1) * 128],
                        B[64 * half:64 * half + 64, 2048 + c * TC:2048 + (c + 1) * TC],
                        start=True, stop=True,
                        tile_position=(64 * half, 0)), inc=(1 if c == 2 else 0))
                pr.wait_tok("s", pr.tok("t"))
                pr.op("s", lambda s, half=half, hh=hh: s.activation(
                    B[:, 4096 + 2048 * half: 4096 + 2048 * half + 1536],
                    PB[:, 0:1536], AF.Exp, bias=N20[:, :],
                    scale=TMP128[:, hh:hh + 1],
                    accum_out=ZZ[:, 4:5]), inc=1)
                ctx.pbtok[1] = pr.tok("s")
                ctx.pbtok[2] = pr.tok("s")
                pr.wait_tok("t", pr.tok("s"))
                pr.op("t", lambda t, half=half, tt=tt: t.matmul(
                    PB[:, 0:TC],
                    B[64 * half:64 * half + 64, tt * 128:(tt + 1) * 128],
                    B[64 * half:64 * half + 64, 2048 + 3 * TC:2048 + 4 * TC],
                    start=True, stop=True,
                    tile_position=(64 * half, 0)), inc=1)
                pr.wait_tok("s", pr.tok("t"))
                pr.op("s", lambda s, half=half, hh=hh: s.activation(
                    B[:, 4096 + 2048 * half + 1536: 4096 + 2048 * half + 2048],
                    PB[:, 0:TC], AF.Exp, bias=N20[:, :],
                    scale=TMP128[:, hh:hh + 1],
                    accum_out=ZZ[:, 5:6]), inc=1)
                ctx.pbtok[0] = pr.tok("s")
                pr.wait_tok("v", pr.tok("s"))
                pr.op("v", lambda v: v.tensor_tensor(
                    ZZ[:, 0:1], ZZ[:, 4:5], ZZ[:, 5:6], op=OP.add), inc=1)
                pr.op("v", lambda v: v.reciprocal(ZZ[:, 2:3], ZZ[:, 0:1]), inc=1)
                pr.op("v", lambda v, half=half, omt=omt, par=par, pair=pair: v.tensor_scalar(
                    QLB[:, 64 * half:64 * half + 64],
                    A[:, omt * T + 128 * (2 * pair + half) + par:
                       omt * T + 128 * (2 * pair + half) + par + 127:2],
                    ZZ[:, 2:3], None, op0=OP.mult), inc=1)
                ql = pr.tok("v")
                pr.wait_tok("t", ql)
                for c in range(NCH):
                    pr.op("t", lambda t, c=c, half=half, tt=tt: t.matmul(
                        PA[64 * half:64 * half + 64, c * TC:(c + 1) * TC],
                        QLB[:, 64 * half:64 * half + 64],
                        B[:, 4096 + 2048 * half + c * TC: 4096 + 2048 * half + (c + 1) * TC],
                        start=(tt == 0), stop=(tt == 15),
                        tile_position=(0, 64 * half), skip_group_check=True),
                        inc=(1 if c == NCH - 1 else 0))
                pr.wait_tok("s", pr.tok("t"))
                pr.wait_tok("v", pr.tok("t"))
            pr.wait_tok("v", pr.tok("t"))
            for c in range(NCH):
                pr.op("v", lambda v, c=c, half=half: v.tensor_copy(
                    KVST[64 * half:64 * half + 64, c * TC:(c + 1) * TC],
                    PA[64 * half:64 * half + 64, c * TC:(c + 1) * TC]), inc=1)
            pr.wait_tok("t", pr.tok("v"))
        ev = pr.tok("v")
        pr.wait_tok("y", ev)
        pr.op("y", lambda y, pair=pair: y.dma_start(
            out=pd.rearrange("(pr hh dh par dd) -> pr (hh dh) par dd",
                             pr=8, hh=2, dh=64, par=2)[pair],
            in_=KVST[:, :].rearrange("p (par dd) -> p par dd", par=2)), inc=16)

    pr.wait("y", "v")
    pr.wait("y", "dy")
    PBT = ctx.ps["PBT"]
    idn = sb["IDENB"]
    GSC = sb["GSC"]
    land_guard = None
    for tt in range(16):
        pr.wait_tok("y", land_guard)
        ld = pr.op("y", lambda y, tt=tt: y.dma_start(
            out=GSC[:, :],
            in_=pd.rearrange("(tt p d) -> tt p d", tt=16, p=128)[tt]), inc=16)
        pr.wait_tok("t", ld)
        pr.wait_tok("t", ctx.pbtok["bt"])
        for j in range(NJ):
            pr.op("t", lambda t, j=j: t.transpose(
                PBT[:, j * 128:(j + 1) * 128],
                GSC[:, j * 128:(j + 1) * 128], idn[:, :]),
                inc=(1 if j == NJ - 1 else 0))
        land_guard = pr.tok("t")
        pr.wait_tok("v", pr.tok("t"))
        for j in range(NJ):
            pr.op("v", lambda v, j=j, tt=tt: v.scalar_tensor_tensor(
                RX[:, j * T + tt * 128: j * T + (tt + 1) * 128],
                PBT[:, j * 128:(j + 1) * 128], 1.0,
                RX[:, j * T + tt * 128: j * T + (tt + 1) * 128],
                op0=OP.mult, op1=OP.add), inc=1)
        ctx.pbtok["bt"] = pr.tok("v")



def emit_shift_cast(ctx, srcname="RX"):
    pr, sb = ctx.pr, ctx.sb
    S, XN = sb[srcname], sb["XN"]
    for j in range(NJ):
        if j < 4:
            pr.op("v", lambda v, j=j: v.tensor_copy(
                XN[:, j * T:(j + 1) * T], S[:, j * T:(j + 1) * T]), inc=1)
        else:
            pr.op("v", lambda v, j=j: v.memset(XN[:, j * T:j * T + 1], 0.0), inc=1)
            pr.op("v", lambda v: v.drain(), raw=True)
            pr.op("v", lambda v, j=j: v.tensor_copy(
                XN[:, j * T + 1:(j + 1) * T], S[:, j * T:(j + 1) * T - 1]), inc=1)
    return pr.tok("v")


def emit_l2_ffn(ctx, coef=0.5, dbg_stop=None):
    pr, sb = ctx.pr, ctx.sb
    RX, XN = ctx.sb["RX"], ctx.sb["XN"]
    A, B = sb["ARA"], sb["ARB"]
    PA, PB = ctx.ps["PA"], ctx.ps["PB"]
    GSC = sb["GSC"]
    SQB, RNB = sb["RSTB"], sb["MRB"]
    ones_f = sb["ONESF1"]
    idnb = sb["IDENB"]
    N20 = sb["NEG20"]
    SRH = sb["STATR"]
    NRA, NRB = SRH[0:1, 0:T], SRH[0:1, T:2 * T]
    rxd = ctx.dram["RXD"]
    TB2 = B[:, 8320:12416]
    MSK0 = 8320
    EB0 = 12416
    VTM0 = 0

    pr.wait("y", "v")
    pr.wait("y", "t")
    sp = pr.op("y", lambda y: y.dma_start(
        out=rxd.rearrange("(j p t) -> p j t", p=128, t=T),
        in_=RX[:, :].rearrange("p (j t) -> p j t", j=NJ)), inc=16)
    pr.wait_tok("v", sp)
    for j in range(NJ):
        pr.op("v", lambda v, j=j: v.tensor_copy(
            RX[:, j * T:(j + 1) * T], XN[:, j * T:(j + 1) * T]), inc=1)

    for l in range(2):
        emit_shift_cast(ctx, "RX")
        pr.wait_tok("t", pr.tok("v"))
        pr.wait_tok("g", pr.tok("v"))

        def ev_qk(mt, tc, bank):
            pr.wait_tok("v", pr.tok("t"))
            return pr.op("v", lambda v, mt=mt, tc=tc, bank=bank: v.tensor_copy(
                A[:, mt * T + tc * TC: mt * T + (tc + 1) * TC], bank), inc=1)

        qkv = ctx.din["L2QKV%d" % l]
        emit_wstat(ctx, qkv, XN[:, :], ev_qk, wcols=1024)

        pr.wait("g", "t")
        pr.wait("g", "v")
        wv = pr.op("g", lambda g, qkv=qkv: g.dma_start(
            out=sb["CTS"][:, 0:4096].rearrange("p (kt o) -> p kt o", o=512),
            in_=qkv.rearrange("(kt p) o -> p kt o", p=128)[:, :, 1024:1536]), inc=16)
        pr.wait_tok("v", pr.tok("t"))
        pr.op("v", lambda v: v.memset(
            B[:, VTM0:VTM0 + 8320].rearrange("p (tt c) -> p tt c", c=520)
            .rearrange("p tt (h c) -> p tt h c", c=65)[:, :, :, 64], 1.0), inc=1)
        pr.op("v", lambda v: v.drain(), raw=True)
        pr.wait_tok("t", wv)
        for tt in range(16):
            bb = tt % 3
            pr.wait_tok("t", ctx.pbtok[bb])
            for j in range(NJ):
                pr.op("t", lambda t, bb=bb, j=j, tt=tt: t.matmul(
                    PB[:, bb * TC:(bb + 1) * TC],
                    XN[:, j * T + tt * 128: j * T + (tt + 1) * 128],
                    sb["CTS"][:, j * 512:(j + 1) * 512],
                    start=(j == 0), stop=(j == NJ - 1)), inc=(1 if j == NJ - 1 else 0))
            pr.wait_tok("v", pr.tok("t"))
            dsl = B[:, VTM0 + tt * 520: VTM0 + (tt + 1) * 520].rearrange(
                "p (h c) -> p h c", c=65)[:, :, 0:64]
            ctx.pbtok[bb] = pr.op("v", lambda v, bb=bb, dsl=dsl: v.tensor_copy(
                dsl, PB[:, bb * TC:(bb + 1) * TC].rearrange("p (h c) -> p h c", c=64)),
                inc=1)

        if dbg_stop == "proj":
            return
        for part, t0t, scn in (("q", 0, "L2QS"), ("k", 4, "L2KS")):
            tb = pr.op("g", lambda g, part=part: g.dma_start(
                out=TB2.rearrange("p (a f) -> p a f", a=2),
                in_=ctx.din["C_CS" + part.upper()].rearrange("(a q) f -> q a f", a=2)),
                inc=16)
            pr.wait_tok("v", tb)
            for ti in range(4):
                tile = t0t + ti
                asl = A[:, tile * T:(tile + 1) * T]
                pr.op("v", lambda v, asl=asl: v.tensor_tensor(
                    SQB[:, :], asl, asl, op=OP.mult), inc=1)
                pr.wait_tok("t", pr.tok("v"))
                for c in range(NCH):
                    bb = c % 3
                    pr.wait_tok("t", ctx.pbtok[bb])
                    pr.op("t", lambda t, c=c, bb=bb: t.matmul(
                        PB[:, bb * TC:(bb + 1) * TC], sb["ONEB2"][:, :],
                        SQB[:, c * TC:(c + 1) * TC],
                        start=True, stop=True), inc=1)
                    pr.wait_tok("v", pr.tok("t"))
                    pr.op("v", lambda v, c=c, bb=bb: v.tensor_copy(
                        NRA[0:1, c * TC:(c + 1) * TC],
                        PB[0:1, bb * TC:(bb + 1) * TC]), inc=1)
                    pr.op("v", lambda v, c=c, bb=bb: v.tensor_copy(
                        NRB[0:1, c * TC:(c + 1) * TC],
                        PB[64:65, bb * TC:(bb + 1) * TC]), inc=1)
                    ctx.pbtok[bb] = pr.tok("v")
                if dbg_stop == "norm" and part == "q" and ti == 0:
                    pr.op("v", lambda v: v.tensor_copy(
                        A[0:1, 4 * T:5 * T], NRA[0:1, :]), inc=1)
                for row in (NRA, NRB):
                    pr.op("v", lambda v, row=row: v.tensor_single_scalar(
                        row, row, 1e-24, op=OP.max), inc=1)
                pr.wait_tok("s", pr.tok("v"))
                for row in (NRA, NRB):
                    pr.op("s", lambda s, row=row: s.activation(row, row, AF.Ln), inc=1)
                    pr.op("s", lambda s, row=row: s.activation(
                        row, row, AF.Exp, scale=sb["NEGH"][0:1, 0:1]), inc=1)
                pr.op("s", lambda s: s.drain(), raw=True)
                pr.wait_tok("t", pr.tok("s"))
                for c in range(NCH):
                    b1, b2 = c % 2, 2
                    pr.wait_tok("t", ctx.pbtok[b1])
                    pr.wait_tok("t", ctx.pbtok[b2])
                    pr.op("t", lambda t, c=c, b1=b1: t.matmul(
                        PB[0:64, b1 * TC:(b1 + 1) * TC], ones_f[0:1, 0:64],
                        NRA[0:1, c * TC:(c + 1) * TC], start=True, stop=True), inc=1)
                    pr.op("t", lambda t, c=c, b2=b2: t.matmul(
                        PB[0:64, b2 * TC:(b2 + 1) * TC], ones_f[0:1, 0:64],
                        NRB[0:1, c * TC:(c + 1) * TC], start=True, stop=True), inc=1)
                    pr.wait_tok("v", pr.tok("t"))
                    pr.op("v", lambda v, c=c, b1=b1: v.tensor_copy(
                        RNB[0:64, c * TC:(c + 1) * TC],
                        PB[0:64, b1 * TC:(b1 + 1) * TC]), inc=1)
                    pr.op("v", lambda v, c=c, b2=b2: v.tensor_copy(
                        RNB[64:128, c * TC:(c + 1) * TC],
                        PB[0:64, b2 * TC:(b2 + 1) * TC]), inc=1)
                    ctx.pbtok[b1] = pr.tok("v")
                    ctx.pbtok[b2] = pr.tok("v")
                pr.op("v", lambda v, asl=asl, scn=scn, l=l: v.scalar_tensor_tensor(
                    asl, asl, sb[scn][:, l:l + 1], RNB[:, :],
                    op0=OP.mult, op1=OP.mult), inc=1)
                if dbg_stop == "norm":
                    if part == "q" and ti == 0:
                        pr.op("v", lambda v: v.tensor_copy(
                            A[:, 7 * T:8 * T], RNB[:, :]), inc=1)
                        pr.op("v", lambda v: v.tensor_copy(
                            A[:, 6 * T:7 * T], SQB[:, :]), inc=1)
                        pr.op("v", lambda v: v.tensor_copy(
                            A[0:1, 5 * T:6 * T], NRA[0:1, :]), inc=1)
                    continue
                for hb_ in range(4):
                    so, do_ = (32 * (hb_ ^ 1), 32 * hb_)
                    pr.op("v", lambda v, so=so, do_=do_, asl=asl: v.tensor_copy(
                        SQB[do_:do_ + 32, 0:T], asl[so:so + 32, :]), inc=1)
                pr.op("v", lambda v, asl=asl: v.tensor_tensor(
                    RNB[:, :], asl, TB2[:, 0:2048], op=OP.mult), inc=1)
                pr.op("v", lambda v, asl=asl: v.tensor_tensor(
                    asl, SQB[:, :], TB2[:, 2048:4096], op=OP.mult), inc=1)
                pr.op("v", lambda v, asl=asl: v.tensor_tensor(
                    asl, asl, RNB[:, :], op=OP.add), inc=1)
            pr.wait_tok("g", pr.tok("v"))
            if dbg_stop == "norm":
                break
        if dbg_stop in ("rotary", "norm"):
            return

        pr.wait("g", "t")
        pr.wait("g", "v")
        mk = pr.op("g", lambda g: g.dma_start(
            out=B[:, MSK0:MSK0 + 4096], in_=ctx.din["C_MKA"][:, :]), inc=16)
        pr.wait_tok("t", mk)
        pa_tok = {0: None, 1: None}
        for h in range(8):
            tile, base = h // 2, 64 * (h % 2)
            ktile = 4 + h // 2
            for w in range(4):
                jlist = list(range(4, 8)) if w == 0 else list(range(8))
                pr.wait_tok("t", pa_tok[w % 2])
                for ji, jt in enumerate(jlist):
                    bb = 2 + 0
                    pr.wait_tok("t", ctx.pbtok[2])
                    jtok0 = (w - 1) * 512 + jt * 128
                    pr.op("t", lambda t, jt=jt: t.matmul(
                        PB[:, 2 * TC:3 * TC], idnb[:, :],
                        B[:, MSK0 + jt * TC:MSK0 + (jt + 1) * TC],
                        start=True, stop=False), inc=0)
                    pr.op("t", lambda t, jtok0=jtok0, ktile=ktile, base=base, w=w, tile=tile: t.matmul(
                        PB[:, 2 * TC:3 * TC],
                        A[base:base + 64, ktile * T + jtok0: ktile * T + jtok0 + 128],
                        A[base:base + 64, tile * T + w * TC: tile * T + (w + 1) * TC],
                        start=False, stop=True, tile_position=(base, 0)), inc=1)
                    pr.wait_tok("s", pr.tok("t"))
                    es = ji % 2
                    pr.op("s", lambda s, es=es: s.activation(
                        B[:, EB0 + es * TC:EB0 + (es + 1) * TC],
                        PB[:, 2 * TC:3 * TC], AF.Exp, bias=N20[:, :]), inc=1)
                    ctx.pbtok[2] = pr.tok("s")
                    pr.wait_tok("t", pr.tok("s"))
                    jt_g = 4 * (w - 1) + jt
                    pr.op("t", lambda t, jt_g=jt_g, h=h, es=es, ji=ji, w=w, jlist=jlist: t.matmul(
                        PA[0:65, (w % 2) * TC:(w % 2 + 1) * TC],
                        B[:, VTM0 + jt_g * 520 + h * 65: VTM0 + jt_g * 520 + (h + 1) * 65],
                        B[:, EB0 + es * TC:EB0 + (es + 1) * TC],
                        start=(ji == 0), stop=(ji == len(jlist) - 1)),
                        inc=(1 if ji == len(jlist) - 1 else 0))
                pr.wait_tok("v", pr.tok("t"))
                pr.op("v", lambda v, w=w: v.reciprocal(
                    NRA[0:1, 0:TC], PA[64:65, (w % 2) * TC:(w % 2 + 1) * TC]), inc=1)
                pr.wait_tok("t", pr.tok("v"))
                pr.wait_tok("t", ctx.pbtok[0])
                pr.op("t", lambda t: t.matmul(
                    PB[0:64, 0:TC], ones_f[0:1, 0:64], NRA[0:1, 0:TC],
                    start=True, stop=True), inc=1)
                pr.wait_tok("v", pr.tok("t"))
                pr.op("v", lambda v: v.tensor_copy(GSC[0:64, 0:TC], PB[0:64, 0:TC]), inc=1)
                ctx.pbtok[0] = pr.tok("v")
                pr.op("v", lambda v, w=w, tile=tile, base=base: v.tensor_tensor(
                    A[base:base + 64, tile * T + w * TC: tile * T + (w + 1) * TC],
                    PA[0:64, (w % 2) * TC:(w % 2 + 1) * TC],
                    GSC[0:64, 0:TC], op=OP.mult), inc=1)
                pa_tok[w % 2] = pr.tok("v")
                pr.wait_tok("t", pr.tok("v"))

        def ev_wo(mt, tc, bank):
            pr.wait_tok("v", pr.tok("t"))
            sl = slice(mt * T + tc * TC, mt * T + (tc + 1) * TC)
            return pr.op("v", lambda v, bank=bank, sl=sl: v.scalar_tensor_tensor(
                RX[:, sl], bank, 1.0, RX[:, sl], op0=OP.mult, op1=OP.add), inc=1)

        wo = ctx.din["L2WO%d" % l]
        emit_wstat(ctx, wo, A[:, 0:4 * T], ev_wo, n_kt=4)
        if dbg_stop in ("wo", "wo%d" % l):
            return

        emit_shift_cast(ctx, "RX")
        pr.wait_tok("t", pr.tok("v"))
        if dbg_stop in ("shift2", "shift2%d" % l):
            for j in range(NJ):
                pr.op("v", lambda v, j=j: v.tensor_copy(
                    RX[:, j * T:(j + 1) * T], XN[:, j * T:(j + 1) * T]), inc=1)
            return
        emit_glue_ln(ctx, src="XN", dst="XN", gam="L2G%d" % l, bet="L2B%d" % l,
                     src_bf16=True)
        pr.barrier()
        if dbg_stop == "rxln" and l == 1:
            return
        if dbg_stop in ("ln2", "ln2%d" % l):
            for j in range(NJ):
                pr.op("v", lambda v, j=j: v.tensor_copy(
                    RX[:, j * T:(j + 1) * T], XN[:, j * T:(j + 1) * T]), inc=1)
            return
        w1 = ctx.din["L2W1P%d" % l]
        w2 = ctx.din["L2W2%d" % l]
        for th in range(2):
            wbufs = (sb["CT0"], sb["CT1"])
            wfree = [None, None]
            pr.wait("g", "t")
            pr.wait("g", "v")
            pr.wait("g", "s")
            for mt in range(44):
                if dbg_stop == "xnw1" and l == 1 and DBG_MT is not None and mt >= DBG_MT:
                    break
                rows = 128 if mt < 42 else 42
                col0 = 128 * mt if mt < 42 else 5376 + 42 * (mt - 42)
                wb = wbufs[mt % 2]
                pr.wait_tok("g", wfree[mt % 2])
                src = w1.rearrange("(kt p) o -> p kt o", p=128)[:, :, col0:col0 + rows]
                wtok = pr.op("g", lambda g, wb=wb, src=src, rows=rows: g.dma_start(
                    out=wb[:, 0:8 * rows].rearrange("p (kt o) -> p kt o", o=rows),
                    in_=src), inc=16)
                pr.wait_tok("t", wtok)
                kt = mt // 2
                h1dst = (A[:, kt * 1024:] if kt < 16 else B[:, (kt - 16) * 1024:])
                for tc2 in range(2):
                    bb = (mt * 2 + tc2) % 3
                    pr.wait_tok("t", ctx.pbtok[bb])
                    bank = PB[0:rows, bb * TC:(bb + 1) * TC]
                    for ktt in range(NJ):
                        pr.op("t", lambda t, bank=bank, wb=wb, ktt=ktt, tc2=tc2, th=th, rows=rows: t.matmul(
                            bank, wb[:, ktt * rows:(ktt + 1) * rows],
                            XN[:, ktt * T + th * 1024 + tc2 * TC:
                               ktt * T + th * 1024 + (tc2 + 1) * TC],
                            start=(ktt == 0), stop=(ktt == NJ - 1)),
                            inc=(1 if ktt == NJ - 1 else 0))
                    dcol = tc2 * TC
                    if mt % 2 == 0:
                        pr.wait_tok("v", pr.tok("t"))
                        ctx.pbtok[bb] = pr.op("v", lambda v, bank=bank, mt=mt, rows=rows, tc2=tc2, l=l: v.tensor_scalar(
                            GSC[0:rows, tc2 * TC:(tc2 + 1) * TC], bank,
                            sb["L2B1A"][0:rows, 22 * l + mt // 2: 22 * l + mt // 2 + 1],
                            None, op0=OP.add), inc=1)
                    else:
                        pr.wait_tok("s", pr.tok("t"))
                        pr.op("s", lambda s, bank=bank, mt=mt, rows=rows, h1dst=h1dst, dcol=dcol, l=l: s.activation(
                            h1dst[0:rows, dcol:dcol + TC], bank, AF.Gelu,
                            bias=sb["L2B1G"][0:rows, 22 * l + mt // 2: 22 * l + mt // 2 + 1]), inc=1)
                        ctx.pbtok[bb] = pr.tok("s")
                        pr.wait_tok("v", pr.tok("s"))
                        pr.op("v", lambda v, rows=rows, h1dst=h1dst, dcol=dcol, tc2=tc2: v.tensor_tensor(
                            h1dst[0:rows, dcol:dcol + TC], h1dst[0:rows, dcol:dcol + TC],
                            GSC[0:rows, tc2 * TC:(tc2 + 1) * TC], op=OP.mult), inc=1)
                wfree[mt % 2] = pr.tok("t")
            if dbg_stop == "xnw1" and l == 1:
                pr.wait_tok("v", pr.tok("s"))
                for j in range(NJ):
                    pr.op("v", lambda v, j=j: v.tensor_copy(
                        RX[:, j * T:(j + 1) * T], XN[:, j * T:(j + 1) * T]), inc=1)
                return
            if dbg_stop in ("h1", "h1%d" % l):
                for kt in range(8):
                    pr.op("v", lambda v, kt=kt: v.tensor_copy(
                        RX[:, kt * T:kt * T + 1024],
                        A[:, kt * 1024:(kt + 1) * 1024]), inc=1)
                return
            if dbg_stop == "h1b":
                for j in range(8):
                    pr.op("v", lambda v, j=j: v.tensor_copy(
                        RX[:, j * T:j * T + 1024],
                        A[:, (8 + j) * 1024:(9 + j) * 1024]), inc=1)
                for j in range(6):
                    pr.op("v", lambda v, j=j: v.tensor_copy(
                        RX[:, j * T + 1024:(j + 1) * T],
                        B[:, j * 1024:(j + 1) * 1024]), inc=1)
                return
            h1_tok = pr.tok("v")
            pr.wait_tok("t", h1_tok)
            pr.wait("g", "t")
            pr.wait("g", "v")
            for ot in range(8):
                pr.wait("g", "t")
                pr.op("g", lambda g, ot=ot, w2=w2: g.dma_start(
                    out=sb["CT0"][:, 0:1408].rearrange("p (kt o) -> p kt o", o=128),
                    in_=w2[0:1408, ot * 128:(ot + 1) * 128].rearrange(
                        "(kt p) o -> p kt o", p=128)), inc=16)
                pr.op("g", lambda g, ot=ot, w2=w2: g.dma_start(
                    out=sb["CT1"][:, 0:1280].rearrange("p (kt o) -> p kt o", o=128),
                    in_=w2[1408:2688, ot * 128:(ot + 1) * 128].rearrange(
                        "(kt p) o -> p kt o", p=128)), inc=16)
                wtok = pr.op("g", lambda g, ot=ot, w2=w2: g.dma_start(
                    out=sb["CT1"][0:42, 1280:1408],
                    in_=w2[2688:2730, ot * 128:(ot + 1) * 128]), inc=16)
                pr.wait_tok("t", wtok)
                for tc2 in range(2):
                    bb = tc2
                    pr.wait_tok("t", ctx.pbtok[bb])
                    bank = PB[:, bb * TC:(bb + 1) * TC]
                    for kt in range(22):
                        rows = 128 if kt < 21 else 42
                        wb = sb["CT0"] if kt < 11 else sb["CT1"]
                        wcol = (kt % 11) * 128
                        h1src = (A[:, kt * 1024:] if kt < 16 else B[:, (kt - 16) * 1024:])
                        pr.op("t", lambda t, bank=bank, wb=wb, kt=kt, rows=rows, wcol=wcol, h1src=h1src, tc2=tc2: t.matmul(
                            bank, wb[0:rows, wcol:wcol + 128],
                            h1src[0:rows, tc2 * TC:(tc2 + 1) * TC],
                            start=(kt == 0), stop=(kt == 21)),
                            inc=(1 if kt == 21 else 0))
                pr.wait_tok("v", pr.tok("t"))
                for tc2 in range(2):
                    bank = PB[:, tc2 * TC:(tc2 + 1) * TC]
                    sl = slice(ot * T + th * 1024 + tc2 * TC,
                               ot * T + th * 1024 + (tc2 + 1) * TC)
                    ctx.pbtok[tc2] = pr.op("v", lambda v, bank=bank, sl=sl, ot=ot, l=l: v.scalar_tensor_tensor(
                        RX[:, sl], bank, sb["L2B2"][:, 8 * l + ot:8 * l + ot + 1],
                        RX[:, sl], op0=OP.add, op1=OP.add), inc=1)
            if dbg_stop in ("w2h0", "w2h0%d" % l) and th == 0:
                return
        if dbg_stop in ("mlp", "mlp%d" % l):
            return

    pr.wait_tok("v", pr.tok("t"))
    for j in range(NJ):
        pr.op("v", lambda v, j=j, coef=coef: v.tensor_single_scalar(
            RX[:, j * T:(j + 1) * T], RX[:, j * T:(j + 1) * T], coef, op=OP.mult), inc=1)
    pr.wait_tok("g", pr.tok("v"))
    acc = pr.op("g", lambda g: g.dma_start(
        out=RX[:, :].rearrange("p (j t) -> p j t", j=NJ),
        in_=rxd.rearrange("(j p t) -> p j t", p=128, t=T),
        accum_op=OP.add), inc=16)
    pr.wait_tok("v", acc)
    pr.wait_tok("t", acc)



def emit_l2_attn(ctx):
    pr, sb = ctx.pr, ctx.sb
    RX, XN = sb["RX"], sb["XN"]
    A, B = sb["ARA"], sb["ARB"]
    PA, PB = ctx.ps["PA"], ctx.ps["PB"]
    GSC, BCB = sb["GSC"], sb["BCB"]
    KTs, VTs = sb["CT0"], sb["CT1"]
    ones_f = sb["ONESF1"]

    def ev_q(mt, tc, bank):
        pr.wait_tok("v", pr.tok("t"))
        return pr.op("v", lambda v, mt=mt, tc=tc, bank=bank: v.tensor_scalar(
            A[:, mt * T + tc * TC: mt * T + (tc + 1) * TC], bank,
            sb["QB2"][:, mt:mt + 1], None, op0=OP.add), inc=1)

    emit_wstat(ctx, ctx.din["WQKV2Q"], XN[:, :], ev_q)

    for mt in range(8):
        for hc in range(2):
            sl = slice(mt * T + hc * 1024, mt * T + (hc + 1) * 1024)
            pr.wait_tok("s", pr.tok("v"))
            pr.op("s", lambda s, sl=sl: s.activation(GSC[:, :], A[:, sl], AF.Exp), inc=1)
            pr.op("s", lambda s: s.drain(), raw=True)
            pr.wait_tok("v", pr.tok("s"))
            pr.op("v", lambda v, sl=sl: v.tensor_single_scalar(
                A[:, sl], A[:, sl], 0.0, op=OP.max), inc=1)
            pr.op("v", lambda v, sl=sl: v.scalar_tensor_tensor(
                A[:, sl], GSC[:, :], 1.0, A[:, sl], op0=OP.min, op1=OP.add), inc=1)
    q_tok = pr.tok("v")

    pr.wait("g", "t")
    pr.wait("g", "v")
    wk = pr.op("g", lambda g: g.dma_start(
        out=B[:, 0:8192].rearrange("p (kt o) -> p kt o", o=1024),
        in_=ctx.din["WQKV2K"].rearrange("(kt p) o -> p kt o", p=128)), inc=16)
    wv = pr.op("g", lambda g: g.dma_start(
        out=B[:, 8192:16384].rearrange("p (kt o) -> p kt o", o=1024),
        in_=ctx.din["WQKV2V"].rearrange("(kt p) o -> p kt o", p=128)), inc=16)
    for brow, dst in (("KB2r", sb["BCB"][:, 0:1024]), ("VB2r", sb["RSTB"][:, 0:1024])):
        ldb = pr.op("y", lambda y, brow=brow: y.dma_start(
            out=sb["L1BKr"][0:1, :], in_=ctx.din[brow][:, :]), inc=16)
        pr.wait_tok("t", ldb)
        for oc in range(2):
            bb = oc
            pr.wait_tok("t", ctx.pbtok[bb])
            pr.op("t", lambda t, oc=oc, bb=bb: t.matmul(
                PB[:, bb * TC:(bb + 1) * TC], ones_f[:, :],
                sb["L1BKr"][0:1, oc * TC:(oc + 1) * TC], start=True, stop=True), inc=1)
        pr.wait_tok("v", pr.tok("t"))
        for oc in range(2):
            ctx.pbtok[oc] = pr.op("v", lambda v, oc=oc, dst=dst: v.tensor_copy(
                dst[:, oc * TC:(oc + 1) * TC], PB[:, oc * TC:(oc + 1) * TC]), inc=1)

    pr.wait_tok("v", pr.tok("t"))
    for buf in (KTs, VTs):
        pr.op("v", lambda v, buf=buf: v.memset(
            buf[:, 0:1040].rearrange("p (h c) -> p h c", c=65)[:, :, 64], 1.0), inc=1)
    pr.op("v", lambda v: v.drain(), raw=True)
    pr.wait_tok("t", wv)

    for tt in range(16):
        for part, wof in (("k", 0), ("v", 8192)):
            for oc in range(2):
                bb = 2 if part == "k" else (oc)
                bb = (2 * (part == "v") + oc) % 3
                pr.wait_tok("t", ctx.pbtok[bb])
                for j in range(NJ):
                    pr.op("t", lambda t, bb=bb, j=j, tt=tt, wof=wof, oc=oc: t.matmul(
                        PB[:, bb * TC:(bb + 1) * TC],
                        XN[:, j * T + tt * 128: j * T + (tt + 1) * 128],
                        B[:, wof + j * 1024 + oc * TC: wof + j * 1024 + (oc + 1) * TC],
                        start=(j == 0), stop=(j == NJ - 1)), inc=(1 if j == NJ - 1 else 0))
                mmtok = pr.tok("t")
                dstb = (KTs if part == "k" else VTs)
                dsl = dstb[:, 0:1040].rearrange("p (h c) -> p h c", c=65)[
                    :, oc * 8:(oc + 1) * 8, 0:64]
                if part == "k":
                    pr.wait_tok("v", mmtok)
                    pr.op("v", lambda v, bb=bb, oc=oc: v.tensor_tensor(
                        GSC[:, 0:TC], PB[:, bb * TC:(bb + 1) * TC],
                        BCB[:, oc * TC:(oc + 1) * TC], op=OP.add), inc=1)
                    ctx.pbtok[bb] = pr.tok("v")
                    pr.wait_tok("s", pr.tok("v"))
                    pr.op("s", lambda s: s.activation(GSC[:, 512:1024],
                                                      GSC[:, 0:TC], AF.Exp), inc=1)
                    pr.op("s", lambda s: s.drain(), raw=True)
                    pr.wait_tok("v", pr.tok("s"))
                    pr.op("v", lambda v, dsl=dsl: v.tensor_single_scalar(
                        GSC[:, 0:TC], GSC[:, 0:TC], 0.0, op=OP.max), inc=1)
                    pr.op("v", lambda v, dsl=dsl: v.scalar_tensor_tensor(
                        dsl, GSC[:, 512:1024].rearrange("p (h c) -> p h c", c=64),
                        1.0, GSC[:, 0:TC].rearrange("p (h c) -> p h c", c=64),
                        op0=OP.min, op1=OP.add), inc=1)
                else:
                    pr.wait_tok("v", mmtok)
                    ctx.pbtok[bb] = pr.op("v", lambda v, bb=bb, oc=oc, dsl=dsl: v.tensor_tensor(
                        dsl,
                        PB[:, bb * TC:(bb + 1) * TC].rearrange("p (h c) -> p h c", c=64),
                        sb["RSTB"][:, oc * TC:(oc + 1) * TC].rearrange(
                            "p (h c) -> p h c", c=64), op=OP.add), inc=1)
        st_tok = pr.tok("v")
        pr.wait_tok("t", st_tok)
        for h in range(16):
            pr.op("t", lambda t, h=h, tt=tt: t.matmul(
                PA[0:65, (h // 4) * TC + (h % 4) * 65:(h // 4) * TC + (h % 4) * 65 + 65],
                KTs[:, h * 65:(h + 1) * 65], VTs[:, h * 65:(h + 1) * 65],
                start=(tt == 0 and h % 4 == 0), stop=(tt == 15 and h % 4 == 3),
                skip_group_check=True), inc=(1 if h == 15 else 0))
        pr.wait_tok("v", pr.tok("t"))

    pr.wait_tok("v", pr.tok("t"))
    for h in range(16):
        src = PA[0:64, (h // 4) * TC + (h % 4) * 65:(h // 4) * TC + (h % 4) * 65 + 65]
        pr.op("v", lambda v, h=h, src=src: v.tensor_copy(
            KTs[0:64, h * 65:(h + 1) * 65], src), inc=1)
        pr.op("v", lambda v, h=h, src=src: v.tensor_copy(
            KTs[64:128, h * 65:(h + 1) * 65], src), inc=1)
    kve_tok = pr.tok("v")
    pr.wait_tok("t", kve_tok)

    SRH = ctx.sb["STATR"]
    ZR2 = SRH[0:1, 0:TC]
    for h in range(16):
        base = 64 * (h % 2)
        for c in range(NCH):
            bb = c % 2
            pr.wait_tok("t", ctx.pbtok[bb])
            pr.op("t", lambda t, h=h, c=c, bb=bb, base=base: t.matmul(
                PB[0:65, bb * TC:(bb + 1) * TC],
                KTs[base:base + 64, h * 65:(h + 1) * 65],
                A[base:base + 64, (h // 2) * T + c * TC:(h // 2) * T + (c + 1) * TC],
                start=True, stop=True, tile_position=(base, 0)), inc=1)
            pr.wait_tok("v", pr.tok("t"))
            pr.op("v", lambda v, bb=bb: v.tensor_single_scalar(
                ZR2, PB[64:65, bb * TC:(bb + 1) * TC], 1e-8, op=OP.add), inc=1)
            pr.op("v", lambda v: v.reciprocal(ZR2, ZR2), inc=1)
            zt = pr.tok("v")
            bb2 = 2
            pr.wait_tok("t", zt)
            pr.wait_tok("t", ctx.pbtok[bb2])
            pr.op("t", lambda t, bb2=bb2: t.matmul(
                PB[0:64, bb2 * TC:(bb2 + 1) * TC], ones_f[0:1, 0:64], ZR2,
                start=True, stop=True), inc=1)
            pr.wait_tok("v", pr.tok("t"))
            pr.op("v", lambda v, bb2=bb2: v.tensor_copy(
                GSC[0:64, 0:TC], PB[0:64, bb2 * TC:(bb2 + 1) * TC]), inc=1)
            ctx.pbtok[bb2] = pr.tok("v")
            pr.op("v", lambda v, h=h, c=c, bb=bb, base=base: v.tensor_tensor(
                A[base:base + 64, (h // 2) * T + c * TC:(h // 2) * T + (c + 1) * TC],
                PB[0:64, bb * TC:(bb + 1) * TC], GSC[0:64, 0:TC], op=OP.mult), inc=1)
            ctx.pbtok[bb] = pr.tok("v")

    def ev_p(mt, tc, bank):
        pr.wait_tok("v", pr.tok("t"))
        sl = slice(mt * T + tc * TC, mt * T + (tc + 1) * TC)
        return pr.op("v", lambda v, bank=bank, sl=sl, mt=mt: v.scalar_tensor_tensor(
            RX[:, sl], bank, sb["PB2"][:, mt:mt + 1], RX[:, sl],
            op0=OP.add, op1=OP.add), inc=1)

    emit_wstat(ctx, ctx.din["WPRJ2"], A[:, :], ev_p)



def emit_l3_attn(ctx):
    pr, sb = ctx.pr, ctx.sb
    RX, XN = sb["RX"], sb["XN"]
    A, B = sb["ARA"], sb["ARB"]
    PA, PB = ctx.ps["PA"], ctx.ps["PB"]
    N20 = sb["NEG20"]
    ones_f = sb["ONESF1"]
    GSC = sb["GSC"]
    SR = sb["STATR"]
    ZR = SR[0:1, 0:T]

    ldv = pr.op("y", lambda y: y.dma_start(out=sb["VB3r"][0:1, :],
                                           in_=ctx.din["VB3r"][:, :]), inc=16)
    pr.wait_tok("t", ldv)
    VTM0 = 0
    KDUP0 = 4224
    EB0 = 6272

    KT0 = 8192

    def ev_qk(mt, tc, bank):
        pr.wait_tok("v", pr.tok("t"))
        if mt < 8:
            dsl = A[:, mt * T + tc * TC: mt * T + (tc + 1) * TC]
        else:
            dsl = B[:, KT0 + (mt - 8) * T + tc * TC: KT0 + (mt - 8) * T + (tc + 1) * TC]
        return pr.op("v", lambda v, dsl=dsl, mt=mt, bank=bank: v.tensor_scalar(
            dsl, bank, sb["QB3"][:, mt:mt + 1], None, op0=OP.add), inc=1)

    emit_wstat(ctx, ctx.din["QKV3T"], XN[:, :], ev_qk, n_mt=10, wcols=1536)

    pr.wait("g", "t")
    pr.wait("g", "v")
    wv = pr.op("g", lambda g: g.dma_start(
        out=sb["CT0"][:, :].rearrange("p (kt o) -> p kt o", o=256),
        in_=ctx.din["QKV3T"].rearrange("(kt p) o -> p kt o", p=128)[:, :, 1280:1536]), inc=16)
    bb = 0
    pr.wait_tok("t", ctx.pbtok[bb])
    bank = PB[:, bb * TC:(bb + 1) * TC]
    pr.op("t", lambda t, bank=bank: t.matmul(
        bank[:, 0:256], ones_f[:, :], sb["VB3r"][0:1, :], start=True, stop=True), inc=1)
    pr.wait_tok("v", pr.tok("t"))
    pr.op("v", lambda v, bank=bank: v.tensor_copy(GSC[:, 0:256], bank[:, 0:256]), inc=1)
    ctx.pbtok[bb] = pr.tok("v")
    for kv in range(4):
        pr.op("v", lambda v, kv=kv: v.memset(
            B[:, VTM0:VTM0 + 4160].rearrange("p (tt c) -> p tt c", c=260)[:, :, kv * 65 + 64],
            1.0), inc=1)
    pr.op("v", lambda v: v.drain(), raw=True)
    pr.wait_tok("t", wv)
    for tt in range(16):
        bb = 1 + tt % 2
        pr.wait_tok("t", ctx.pbtok[bb])
        bank = PB[:, bb * TC:(bb + 1) * TC]
        for j in range(NJ):
            pr.op("t", lambda t, bank=bank, j=j, tt=tt: t.matmul(
                bank[:, 0:256], XN[:, j * T + tt * 128: j * T + (tt + 1) * 128],
                sb["CT0"][:, j * 256:(j + 1) * 256],
                start=(j == 0), stop=(j == NJ - 1)), inc=(1 if j == NJ - 1 else 0))
        pr.wait_tok("v", pr.tok("t"))
        dst = B[:, VTM0 + tt * 260: VTM0 + (tt + 1) * 260]
        dst = dst.rearrange("p (kv c) -> p kv c", c=65)[:, :, 0:64]
        ctx.pbtok[bb] = pr.op("v", lambda v, bank=bank, dst=dst: v.tensor_tensor(
            dst, bank[:, 0:256].rearrange("p (kv c) -> p kv c", c=64),
            GSC[:, 0:256].rearrange("p (kv c) -> p kv c", c=64), op=OP.add), inc=1)

    SCL = sb["SCL64"]
    pr.wait_tok("s", pr.tok("v"))
    pr.op("s", lambda s: s.activation(SCL[0:64, 1:2], SCL[0:64, 0:1], AF.Abs), inc=1)
    pr.op("s", lambda s: s.activation(SCL[0:64, 1:2], SCL[0:64, 1:2], AF.Exp,
                                      scale=sb["NEGONE"][0:64, :]), inc=1)
    pr.op("s", lambda s: s.activation(SCL[0:64, 1:2], SCL[0:64, 1:2], AF.Ln,
                                      bias=sb["ONEC"][0:64, :]), inc=1)
    pr.op("s", lambda s: s.drain(), raw=True)
    pr.wait_tok("v", pr.tok("s"))
    pr.op("v", lambda v: v.tensor_single_scalar(SCL[0:64, 0:1], SCL[0:64, 0:1],
                                                0.0, op=OP.max), inc=1)
    pr.op("v", lambda v: v.tensor_tensor(SCL[0:64, 0:1], SCL[0:64, 0:1],
                                         SCL[0:64, 1:2], op=OP.add), inc=1)
    pr.op("v", lambda v: v.tensor_single_scalar(SCL[0:64, 0:1], SCL[0:64, 0:1],
                                                1.442695041 / 8.0, op=OP.mult), inc=1)
    pr.op("v", lambda v: v.tensor_copy(SCL[64:128, 0:1], SCL[0:64, 0:1]), inc=1)
    sc_tok = pr.tok("v")
    pr.wait_tok("v", sc_tok)
    for mt in range(8):
        pr.op("v", lambda v, mt=mt: v.tensor_scalar(
            A[:, mt * T:(mt + 1) * T], A[:, mt * T:(mt + 1) * T],
            SCL[:, 0:1], None, op0=OP.mult), inc=1)
    qs_tok = pr.tok("v")
    pr.wait_tok("t", qs_tok)

    for g in range(4):
        kt_tile = g // 2
        kbase = 64 * (g % 2)
        pr.op("v", lambda v, kt_tile=kt_tile, kbase=kbase: v.tensor_copy(
            B[0:64, KDUP0:KDUP0 + 2048],
            B[kbase:kbase + 64, KT0 + kt_tile * T:KT0 + (kt_tile + 1) * T]), inc=1)
        pr.op("v", lambda v, kt_tile=kt_tile, kbase=kbase: v.tensor_copy(
            B[64:128, KDUP0:KDUP0 + 2048],
            B[kbase:kbase + 64, KT0 + kt_tile * T:KT0 + (kt_tile + 1) * T]), inc=1)
        pr.wait_tok("t", pr.tok("v"))
        for h in range(4):
            hg = 4 * g + h
            qt_tile, qbase = hg // 2, 64 * (hg % 2)
            for ic in range(NCH):
                for jt in range(16):
                    bb = 1 + jt % 2
                    pr.wait_tok("t", ctx.pbtok[bb])
                    pr.op("t", lambda t, jt=jt, ic=ic, qt_tile=qt_tile, qbase=qbase, bb=bb: t.matmul(
                        PB[:, bb * TC:(bb + 1) * TC],
                        B[qbase:qbase + 64, KDUP0 + jt * 128:KDUP0 + (jt + 1) * 128],
                        A[qbase:qbase + 64, qt_tile * T + ic * TC: qt_tile * T + (ic + 1) * TC],
                        start=True, stop=True, tile_position=(qbase, 0)), inc=1)
                    pr.wait_tok("s", pr.tok("t"))
                    eslot = jt % 2
                    pr.op("s", lambda s, bb=bb, eslot=eslot: s.activation(
                        B[:, EB0 + eslot * TC: EB0 + (eslot + 1) * TC],
                        PB[:, bb * TC:(bb + 1) * TC], AF.Exp, bias=N20[:, :]), inc=1)
                    ctx.pbtok[bb] = pr.tok("s")
                    pr.wait_tok("t", pr.tok("s"))
                    pr.op("t", lambda t, jt=jt, ic=ic, g=g, eslot=eslot: t.matmul(
                        PA[0:65, ic * TC:(ic + 1) * TC],
                        B[:, VTM0 + jt * 260 + g * 65: VTM0 + jt * 260 + (g + 1) * 65],
                        B[:, EB0 + eslot * TC: EB0 + (eslot + 1) * TC],
                        start=(jt == 0), stop=(jt == 15)), inc=(1 if jt == 15 else 0))
            pr.wait_tok("v", pr.tok("t"))
            pr.op("v", lambda v: v.reciprocal(ZR, PA[64:65, 0:T]), inc=1)
            for c in range(NCH):
                bb = 1 + c % 2
                pr.wait_tok("t", pr.tok("v"))
                pr.wait_tok("t", ctx.pbtok[bb])
                pr.op("t", lambda t, c=c, bb=bb: t.matmul(
                    PB[0:64, bb * TC:(bb + 1) * TC], ones_f[0:1, 0:64],
                    ZR[0:1, c * TC:(c + 1) * TC], start=True, stop=True), inc=1)
                pr.wait_tok("v", pr.tok("t"))
                pr.op("v", lambda v, c=c, bb=bb: v.tensor_copy(
                    GSC[:, 0:TC][0:64, :], PB[0:64, bb * TC:(bb + 1) * TC]), inc=1)
                ctx.pbtok[bb] = pr.tok("v")
                pr.op("v", lambda v, c=c, qt_tile=qt_tile, qbase=qbase: v.tensor_tensor(
                    A[qbase:qbase + 64, qt_tile * T + c * TC: qt_tile * T + (c + 1) * TC],
                    PA[0:64, c * TC:(c + 1) * TC], GSC[0:64, 0:TC], op=OP.mult), inc=1)
            pr.wait_tok("t", pr.tok("v"))

    def ev_o2(mt, tc, bank):
        pr.wait_tok("v", pr.tok("t"))
        sl = slice(mt * T + tc * TC, mt * T + (tc + 1) * TC)
        return pr.op("v", lambda v, bank=bank, sl=sl, mt=mt: v.scalar_tensor_tensor(
            RX[:, sl], bank, sb["OB3"][:, mt:mt + 1], RX[:, sl],
            op0=OP.add, op1=OP.add), inc=1)

    emit_wstat(ctx, ctx.din["OW3T"], A[:, 0:8 * T], ev_o2)



def emit_l3_ffn(ctx, coef=0.5):
    pr, sb = ctx.pr, ctx.sb
    RX, XN = sb["RX"], sb["XN"]
    A, B = sb["ARA"], sb["ARB"]
    PA, PB = ctx.ps["PA"], ctx.ps["PB"]
    GSC = sb["GSC"]
    SR = sb["STATR"]
    S0R, S1R = SR[0:1, 0:T], SR[0:1, T:2 * T]
    S0B, S1B = sb["RSTB"], sb["MRB"]
    LG8 = sb["LG8"]
    U8T = sb["U8T"]
    NEGT = sb["NEGT"]
    S0C, S1C = sb["S0C"], sb["S1C"]
    ST16 = sb["ST16"]
    ones_f = sb["ONESF1"]
    idnf = sb["IDENFF"]

    pr.op("v", lambda v: v.memset(NEGT[:, :], -3.0e38), inc=1)
    pr.op("v", lambda v: v.drain(), raw=True)
    pr.wait("g", "t")
    pr.wait("g", "v")
    gw = pr.op("g", lambda g: g.dma_start(out=sb["CT1"][:, 0:64],
                                          in_=ctx.din["GW3L"][:, :]), inc=16)
    pr.wait_tok("t", gw)
    for tt in range(16):
        bb = tt % 3
        pr.wait_tok("t", ctx.pbtok[bb])
        bank = PB[:, bb * TC:(bb + 1) * TC]
        for j in range(NJ):
            pr.op("t", lambda t, bank=bank, j=j, tt=tt: t.matmul(
                bank[:, 0:8], XN[:, j * T + tt * 128: j * T + (tt + 1) * 128],
                sb["CT1"][:, j * 8:(j + 1) * 8],
                start=(j == 0), stop=(j == NJ - 1)), inc=(1 if j == NJ - 1 else 0))
        pr.wait_tok("v", pr.tok("t"))
        pr.op("v", lambda v, bank=bank: v.tensor_copy(LG8[:, 0:8], bank[:, 0:8]), inc=1)
        ctx.pbtok[bb] = pr.tok("v")
        pr.op("v", lambda v: v.reduce_max(LG8[:, 24:25], LG8[:, 0:8],
                                          axis=mybir.AxisListType.X), inc=1)
        pr.op("v", lambda v: v.tensor_single_scalar(LG8[:, 25:26], LG8[:, 24:25],
                                                    -1.0, op=OP.mult), inc=1)
        pr.wait_tok("s", pr.tok("v"))
        pr.op("s", lambda s: s.activation(LG8[:, 8:16], LG8[:, 0:8], AF.Exp,
                                          bias=LG8[:, 25:26]), inc=1)
        pr.op("s", lambda s: s.drain(), raw=True)
        pr.wait_tok("v", pr.tok("s"))
        pr.op("v", lambda v: v.reduce_sum(LG8[:, 26:27], LG8[:, 8:16],
                                          axis=mybir.AxisListType.X), inc=1)
        pr.op("v", lambda v: v.reciprocal(LG8[:, 27:28], LG8[:, 26:27]), inc=1)
        pr.op("v", lambda v, coef=coef: v.tensor_scalar(
            LG8[:, 16:24], LG8[:, 8:16], LG8[:, 27:28], coef,
            op0=OP.mult, op1=OP.mult), inc=1)
        pr.op("v", lambda v: v.tensor_scalar(U8T[:, :], LG8[:, 0:8], LG8[:, 24:25],
                                             None, op0=OP.is_equal), inc=1)
        pr.op("v", lambda v: v.select(LG8[:, 8:16], U8T[:, :], NEGT[:, :],
                                      LG8[:, 0:8]), inc=1)
        pr.op("v", lambda v: v.reduce_max(LG8[:, 28:29], LG8[:, 8:16],
                                          axis=mybir.AxisListType.X), inc=1)
        pr.op("v", lambda v: v.tensor_scalar(LG8[:, 29:30], LG8[:, 0:1], LG8[:, 24:25],
                                             None, op0=OP.is_equal), inc=1)
        pr.op("v", lambda v: v.tensor_scalar(LG8[:, 30:31], LG8[:, 9:10], LG8[:, 28:29],
                                             None, op0=OP.is_equal), inc=1)
        pr.op("v", lambda v, tt=tt: v.tensor_tensor(
            S0C[:, tt:tt + 1], LG8[:, 16:17], LG8[:, 29:30], op=OP.mult), inc=1)
        pr.op("v", lambda v, tt=tt: v.tensor_tensor(
            S1C[:, tt:tt + 1], LG8[:, 17:18], LG8[:, 30:31], op=OP.mult), inc=1)
    gate_v = pr.tok("v")
    pr.wait_tok("t", gate_v)
    for which, col, rowdst in ((0, S0C, S0R), (1, S1C, S1R)):
        bb = which
        pr.wait_tok("t", ctx.pbtok[bb])
        pr.op("t", lambda t, col=col, bb=bb: t.transpose(
            PB[0:16, bb * TC:bb * TC + 128], col[:, :], idnf[:, :]), inc=1)
        pr.wait_tok("v", pr.tok("t"))
        pr.op("v", lambda v, bb=bb: v.tensor_copy(ST16[:, :], PB[0:16, bb * TC:bb * TC + 128]), inc=1)
        ctx.pbtok[bb] = pr.tok("v")
        ev = pr.tok("v")
        pr.wait_tok("y", ev)
        pr.op("y", lambda y, rowdst=rowdst: y.dma_start(
            out=rowdst.rearrange("o (a b) -> o a b", a=16),
            in_=ST16[:, :]), inc=16)
        pr.wait("v", "dy")
    for rowsrc, dstb in ((S0R, S0B), (S1R, S1B)):
        pr.wait_tok("t", pr.tok("v"))
        pr.wait_tok("t", pr.tok("y"))
        pr.wait("t", "dy")
        for c in range(NCH):
            bb = c % 3
            pr.wait_tok("t", ctx.pbtok[bb])
            pr.op("t", lambda t, c=c, bb=bb, rowsrc=rowsrc: t.matmul(
                PB[:, bb * TC:(bb + 1) * TC], ones_f[:, :],
                rowsrc[0:1, c * TC:(c + 1) * TC], start=True, stop=True), inc=1)
            pr.wait_tok("v", pr.tok("t"))
            ctx.pbtok[bb] = pr.op("v", lambda v, c=c, bb=bb, dstb=dstb: v.tensor_copy(
                dstb[:, c * TC:(c + 1) * TC], PB[:, bb * TC:(bb + 1) * TC]), inc=1)

    for wname, bname, dst in (("W03T", "B03", A), ("W13T", "B13", B)):
        def ev(mt, tc, bank, bname=bname, dst=dst):
            pr.wait_tok("s", pr.tok("t"))
            return pr.op("s", lambda s, mt=mt, tc=tc, bank=bank: s.activation(
                dst[:, mt * T + tc * TC: mt * T + (tc + 1) * TC], bank, AF.Silu,
                bias=sb[bname][:, mt:mt + 1]), inc=1)
        emit_wstat(ctx, ctx.din[wname], XN[:, :], ev)
    pr.wait_tok("v", pr.tok("s"))

    for j in range(NJ):
        pr.op("v", lambda v, j=j: v.tensor_tensor(
            sb["CT0"][:, :], A[:, j * T:(j + 1) * T], S0B[:, :], op=OP.mult), inc=1)
        pr.op("v", lambda v, j=j: v.scalar_tensor_tensor(
            RX[:, j * T:(j + 1) * T], sb["CT0"][:, :], 1.0,
            RX[:, j * T:(j + 1) * T], op0=OP.mult, op1=OP.add), inc=1)
        pr.op("v", lambda v, j=j: v.tensor_tensor(
            sb["CT0"][:, :], B[:, j * T:(j + 1) * T], S1B[:, :], op=OP.mult), inc=1)
        pr.op("v", lambda v, j=j: v.scalar_tensor_tensor(
            RX[:, j * T:(j + 1) * T], sb["CT0"][:, :], 1.0,
            RX[:, j * T:(j + 1) * T], op0=OP.mult, op1=OP.add), inc=1)



def emit_l0_ffn_prep(ctx, stack):
    nc, pr, sb = ctx.nc, ctx.pr, ctx.sb
    cd = ctx.dram["CDRAM"]
    PA = ctx.ps["PA"]
    with ExitStack() as ps:
        RU = sb["ARA"][0:16, :]
        RM = sb["ARB"][0:16, 0:4096]
        STG = sb["GSC"][0:16, :]
        CBLK = sb["CBLK"]

        pr.op("g", lambda g: g.dma_start(out=RM, in_=ctx.din["RMT"][:, :]), inc=16)
        ld = pr.op("g", lambda g: g.dma_start(out=RU, in_=ctx.din["RUT"][:, :]), inc=16)
        pr.wait_tok("t", ld)
        stg_guard = [None, None]
        for grp in range(32):
            for s8 in range(8):
                hr = grp * 8 + s8
                pr.op("t", lambda t, hr=hr, s8=s8, grp=grp: t.matmul(
                    PA[0:16, (grp % 4) * TC + s8 * 64:(grp % 4) * TC + (s8 + 1) * 64],
                    RM[:, hr * 16:(hr + 1) * 16],
                    RU[:, hr * 64:(hr + 1) * 64],
                    start=True, stop=True), inc=(1 if s8 == 7 else 0))
            pr.wait_tok("v", pr.tok("t"))
            pr.wait_tok("v", stg_guard[grp % 2])
            ev = pr.op("v", lambda v, grp=grp: v.tensor_copy(
                STG[:, (grp % 2) * TC:(grp % 2 + 1) * TC],
                PA[0:16, (grp % 4) * TC:(grp % 4 + 1) * TC]), inc=1)
            pr.wait_tok("y", ev)
            dst = cd.rearrange("(hr i k) -> i hr k", i=16, k=64)[:, grp * 8:(grp + 1) * 8, :]
            stg_guard[grp % 2] = pr.op("y", lambda y, grp=grp, dst=dst: y.dma_start(
                out=dst,
                in_=STG[:, (grp % 2) * TC:(grp % 2 + 1) * TC].rearrange(
                    "i (hr k) -> i hr k", k=64)), inc=16)
            if grp % 4 == 3:
                pr.wait_tok("t", pr.tok("v"))
        pr.op("v", lambda v: v.memset(CBLK[:, :], 0.0), inc=1)
        pr.op("v", lambda v: v.drain(), raw=True)
        zero_v = pr.tok("v")
        pr.wait("y", "dy")
        pr.wait_tok("y", zero_v)
        for rp4 in range(4):
            for m in range(2):
                dst = CBLK[32 * rp4 + 16 * m: 32 * rp4 + 16 * m + 16, :]
                dst = dst.rearrange("i (h q mm k) -> i h q mm k",
                                    h=4, q=8, mm=2)[:, :, :, m, :]
                src = cd.rearrange("(h q rlo i k) -> rlo i h q k",
                                   h=4, q=8, rlo=8, i=16)[2 * rp4 + m]
                pr.op("y", lambda y, dst=dst, src=src: y.dma_start(out=dst, in_=src), inc=16)
        pr.wait("y", "dy")
        cbd = ctx.dram["CBLKD"]
        pr.op("y", lambda y: y.dma_start(
            out=cbd.rearrange("(p f) -> p f", p=128), in_=CBLK[:, :]), inc=16)
        pr.barrier()


def emit_l0_ffn(ctx, coef=0.5):
    pr, sb = ctx.pr, ctx.sb
    XN, RX, CBLK = sb["XN"], sb["RX"], sb["CBLK"]
    H, WB = sb["HBLK"], sb["WBUF"]
    PA = ctx.ps["PA"]
    wct = ctx.din["WCT"]

    pr.wait("y", "t")
    pr.wait("y", "v")
    cb_tok = pr.op("y", lambda y: y.dma_start(
        out=CBLK[:, :], in_=ctx.dram["CBLKD"].rearrange("(p f) -> p f", p=128)), inc=16)
    pr.wait_tok("t", cb_tok)

    wbuf_free = [None, None]
    cproj_banks = ctx.pbtok
    for cb in range(8):
        h_, hb = cb // 2, cb % 2
        w_tok = []
        for wh in range(2):
            pr.wait_tok("g", wbuf_free[wh])
            r0 = (2048 * cb + 1024 * wh) // 128
            src = wct.rearrange("(n p) o -> p n o", p=128)[:, r0:r0 + 8, :]
            w_tok.append(pr.op("g", lambda g, wh=wh, src=src: g.dma_start(
                out=WB[:, wh * 8192:(wh + 1) * 8192].rearrange("p (n o) -> p n o", o=1024),
                in_=src), inc=16))

        for qt in range(4):
            t0 = qt * TC
            for q in range(4):
                j = 4 * hb + q
                colblk = 128 * (8 * h_ + 4 * hb + q)
                for s4 in range(4):
                    pr.op("t", lambda t, s4=s4, j=j, colblk=colblk, t0=t0: t.matmul(
                        PA[:, s4 * TC:(s4 + 1) * TC],
                        CBLK[32 * s4:32 * s4 + 32, colblk:colblk + 128],
                        XN[32 * s4:32 * s4 + 32, j * T + t0: j * T + t0 + TC],
                        start=True, stop=True, tile_position=(32 * s4, 0)),
                        inc=(1 if s4 == 3 else 0))
                pr.wait_tok("s", pr.tok("t"))
                pr.op("s", lambda s_, q=q: s_.activation(
                    H[:, :].rearrange("p (ss f) -> p ss f", ss=16)[:, 4 * q:4 * q + 4, :],
                    PA[:, :].rearrange("p (ss f) -> p ss f", ss=4),
                    AF.Gelu), inc=1)
                pr.wait_tok("t", pr.tok("s"))
            rails_s = pr.tok("s")

            pr.wait_tok("t", w_tok[1])
            pr.wait_tok("t", rails_s)
            for ot in range(8):
                bidx = ot % 3
                pr.wait_tok("t", cproj_banks[bidx])
                bank = bankB(ctx, bidx)
                for ct in range(16):
                    wh, ct8 = ct // 8, ct % 8
                    pr.op("t", lambda t, bank=bank, wh=wh, ct8=ct8, ot=ot, ct=ct: t.matmul(
                        bank,
                        WB[:, wh * 8192 + ct8 * 1024 + ot * 128:
                           wh * 8192 + ct8 * 1024 + (ot + 1) * 128],
                        H[:, ct * TC:(ct + 1) * TC],
                        start=(ct == 0), stop=(ct == 15)), inc=(1 if ct == 15 else 0))
                pr.wait_tok("v", pr.tok("t"))
                sl = slice(ot * T + t0, ot * T + t0 + TC)
                cproj_banks[bidx] = pr.op("v", lambda v, bank=bank, sl=sl: v.scalar_tensor_tensor(
                    RX[:, sl], bank, coef, RX[:, sl], op0=OP.mult, op1=OP.add), inc=1)
        wbuf_free[0] = pr.tok("t")
        wbuf_free[1] = pr.tok("t")



def lay_inputs(inputs):
    f32c = lambda a: np.ascontiguousarray(np.asarray(a, dtype=np.float32))
    L = {}
    rm = np.asarray(inputs["L0_ffn_rail_maker"], np.float32)
    ru = np.asarray(inputs["L0_ffn_rail_up_proj"], np.float32)
    L["RMT"] = f32c(rm.transpose(3, 0, 1, 2).reshape(16, 4 * 64 * 16))
    L["RUT"] = f32c(ru.transpose(3, 0, 1, 2).reshape(16, 4 * 64 * 64))
    L["WCT"] = f32c(np.asarray(inputs["L0_ffn_w_cproj"], np.float32).T)
    for nme, key in [("L0WQT", "L0_attn_Wq"), ("L0WVT", "L0_attn_Wv"),
                     ("L0WGT", "L0_attn_Wg"), ("L0WOT", "L0_attn_Wo")]:
        L[nme] = f32c(np.asarray(inputs[key], np.float32).T)
    beta = np.asarray(inputs["L0_attn_beta"], np.float32)
    p = np.arange(128)
    jj = np.arange(8)
    L["B128"] = f32c(beta[(2 * jj[None, :] + p[:, None] // 64)])
    L["L0G"] = f32c(np.asarray(inputs["L0_attn_ln_gamma"], np.float32).reshape(8, 128).T)
    L["L0B"] = f32c(np.asarray(inputs["L0_attn_ln_beta"], np.float32).reshape(8, 128).T)
    gw = np.asarray(inputs["L1_ffn_gate_w"], np.float32)
    L["L1GWT"] = f32c(gw.T.reshape(8, 128, 8).transpose(1, 0, 2).reshape(128, 64))
    L["L1GBr"] = f32c(np.asarray(inputs["L1_ffn_gate_b"], np.float32).reshape(1, 8))
    eb = np.asarray(inputs["L1_ffn_expert_b"], np.float32)
    L["L1EBL"] = f32c(eb.T.reshape(8, 128, 8).transpose(1, 2, 0).reshape(128, 64))
    ew = np.asarray(inputs["L1_ffn_expert_w"], np.float32)
    L["L1EW"] = f32c(ew.transpose(0, 2, 1).reshape(8 * 1024, 1024))
    L["L1WQ"] = f32c(np.asarray(inputs["L1_attn_Wq"], np.float32))
    L["L1WK"] = f32c(np.asarray(inputs["L1_attn_Wk"], np.float32))
    L["L1WV"] = f32c(np.asarray(inputs["L1_attn_Wv"], np.float32))
    L["L1BQL"] = f32c(np.asarray(inputs["L1_attn_bq"], np.float32).reshape(8, 128).T)
    L["L1BKr"] = f32c(np.asarray(inputs["L1_attn_bk"], np.float32).reshape(1, 1024))
    L["L1BVr"] = f32c(np.asarray(inputs["L1_attn_bv"], np.float32).reshape(1, 1024))
    L["L1TROW"] = f32c(np.asarray(inputs["L1_attn_temperature"], np.float32).reshape(1, 16))
    wq2 = np.asarray(inputs["L2_ffn_wqkv"], np.float32)
    wo2 = np.asarray(inputs["L2_ffn_wo"], np.float32)
    for l_ in range(2):
        L["L2QKV%d" % l_] = f32c(wq2[l_])
        L["L2WO%d" % l_] = f32c(wo2[l_])
    w1f = np.asarray(inputs["L2_ffn_w1"], np.float32)
    b1f = np.asarray(inputs["L2_ffn_b1"], np.float32)
    HID = 2730
    perm = []
    for i in range(21):
        perm.extend(range(128 * i, 128 * (i + 1)))
        perm.extend(range(HID + 128 * i, HID + 128 * (i + 1)))
    perm.extend(range(2688, 2730))
    perm.extend(range(HID + 2688, HID + 2730))
    for l_ in range(2):
        L["L2W1P%d" % l_] = f32c(w1f[l_][:, perm])
    b1a = np.zeros((128, 44), np.float32)
    b1g = np.zeros((128, 44), np.float32)
    for l in range(2):
        for i in range(21):
            b1a[:, 22 * l + i] = b1f[l][128 * i:128 * (i + 1)]
            b1g[:, 22 * l + i] = b1f[l][HID + 128 * i:HID + 128 * (i + 1)]
        b1a[0:42, 22 * l + 21] = b1f[l][2688:2730]
        b1g[0:42, 22 * l + 21] = b1f[l][HID + 2688:HID + 2730]
    L["L2B1A"], L["L2B1G"] = f32c(b1a), f32c(b1g)
    w2f = np.asarray(inputs["L2_ffn_w2"], np.float32)
    for l_ in range(2):
        L["L2W2%d" % l_] = f32c(w2f[l_])
    L["L2B2"] = f32c(np.asarray(inputs["L2_ffn_b2"], np.float32
                                ).reshape(2, 8, 128).transpose(2, 0, 1).reshape(128, 16))
    L["L2GL"] = f32c(np.asarray(inputs["L2_ffn_ln_g"], np.float32
                                ).reshape(2, 8, 128).transpose(2, 0, 1).reshape(128, 16))
    L["L2BL"] = f32c(np.asarray(inputs["L2_ffn_ln_b"], np.float32
                                ).reshape(2, 8, 128).transpose(2, 0, 1).reshape(128, 16))
    qs2 = np.asarray(inputs["L2_ffn_q_scale"], np.float32)
    ks2 = np.asarray(inputs["L2_ffn_k_scale"], np.float32)
    L["L2QSL"] = f32c(np.concatenate([qs2.T, qs2.T], 0))
    L["L2KSL"] = f32c(np.concatenate([ks2.T, ks2.T], 0))
    wqkv2 = np.asarray(inputs["L2_attn_w_qkv"], np.float32)
    L["WQKV2Q"] = f32c(wqkv2[:, 0:1024])
    L["WQKV2K"] = f32c(wqkv2[:, 1024:2048])
    L["WQKV2V"] = f32c(wqkv2[:, 2048:3072])
    L["WPRJ2"] = f32c(np.asarray(inputs["L2_attn_w_proj"], np.float32))
    bq2 = np.asarray(inputs["L2_attn_b_qkv"], np.float32)
    L["QB2L"] = f32c(bq2[0:1024].reshape(8, 128).T)
    L["KB2r"] = f32c(bq2[1024:2048].reshape(1, 1024))
    L["VB2r"] = f32c(bq2[2048:3072].reshape(1, 1024))
    L["PB2L"] = f32c(np.asarray(inputs["L2_attn_b_proj"], np.float32).reshape(8, 128).T)
    L["QKV3T"] = f32c(np.asarray(inputs["L3_attn_qkv_w"], np.float32).T)
    L["OW3T"] = f32c(np.asarray(inputs["L3_attn_o_w"], np.float32).T)
    qb = np.asarray(inputs["L3_attn_qkv_b"], np.float32)
    L["QB3L"] = f32c(qb[0:1280].reshape(10, 128).T)
    L["VB3r"] = f32c(qb[1280:1536].reshape(1, 256))
    L["OB3L"] = f32c(np.asarray(inputs["L3_attn_o_b"], np.float32).reshape(8, 128).T)
    L["L3SCL"] = f32c(np.asarray(inputs["L3_attn_scaling"], np.float32).reshape(64, 1))
    gw3 = np.asarray(inputs["L3_ffn_gate_w"], np.float32)
    L["GW3L"] = f32c(gw3.reshape(8, 128, 8).transpose(1, 0, 2).reshape(128, 64))
    L["W03T"] = f32c(np.asarray(inputs["L3_ffn_w0"], np.float32).T)
    L["W13T"] = f32c(np.asarray(inputs["L3_ffn_w1"], np.float32).T)
    L["B03L"] = f32c(np.asarray(inputs["L3_ffn_b0"], np.float32).reshape(8, 128).T)
    L["B13L"] = f32c(np.asarray(inputs["L3_ffn_b1"], np.float32).reshape(8, 128).T)
    return L


def make_consts():
    c = {}
    c["C_ONESB"] = np.ones((128, 1), np.float32)
    c["C_ONESF1"] = np.ones((1, 128), np.float32)
    c["C_EPS"] = np.full((1, 1), 1e-5, np.float32)
    c["C_NEGH"] = np.full((1, 1), -0.5, np.float32)
    c["C_NEGLN2"] = np.full((128, 1), -0.6931471805599453, np.float32)
    c["C_NEG20"] = np.full((128, 1), -20.0, np.float32)
    c["C_IDENF"] = np.eye(128, dtype=np.float32)
    ob2 = np.zeros((128, 128), np.float32)
    ob2[0:64, 0] = 1.0
    ob2[64:128, 64] = 1.0
    c["C_ONEB2"] = ob2
    jrel = np.arange(1024)[:, None]
    irel = np.arange(512)[None, :]
    mka = np.where((jrel >= irel) & (jrel <= irel + 512), 0.0, -3.0e38).astype(np.float32)
    c["C_MKA"] = np.ascontiguousarray(mka.reshape(8, 128, 512).transpose(1, 0, 2
                                                                         ).reshape(128, 4096))
    DH, WIN, NT = 64, 512, 2048
    half = np.arange(0, DH, 2).astype(np.float64)
    invf = 1.0 / 10000.0 ** (half / DH)
    tt_ = np.arange(NT, dtype=np.float64)
    freqs = tt_[:, None] * invf[None, :]
    freqs = np.concatenate([freqs, freqs], axis=-1)
    base = (half + 0.4 * DH) / (1.4 * DH)
    power = (tt_ - NT // 2) / float(WIN // 2)
    scl = base[None, :] ** power[:, None]
    scl = np.concatenate([scl, scl], axis=-1)
    cosf, sinf = np.cos(freqs), np.sin(freqs)
    sig = np.where(np.arange(DH) < DH // 2, -1.0, 1.0)
    csq = (cosf * scl * 8.0).T
    ssq = (sinf * scl * 8.0 * sig[None, :]).T
    csk = (cosf / scl).T
    ssk = (sinf / scl * sig[None, :]).T
    dup = lambda a: np.concatenate([a, a], 0).astype(np.float32)
    c["C_CSQ"] = np.ascontiguousarray(np.concatenate([dup(csq), dup(ssq)], 0))
    c["C_CSK"] = np.ascontiguousarray(np.concatenate([dup(csk), dup(ssk)], 0))
    return c



def build_program(sublayers):
    nc = bass.Bass("TRN2", target_bir_lowering=False)
    nc.detect_race_conditions = False
    ctx = Ctx(nc)
    pr = ctx.pr
    need = set(sublayers)

    x_t = nc.dram_tensor("x_t", [D, T], F32, kind="ExternalInput").ap()
    y_t = nc.dram_tensor("y_t", [D, T], F32, kind="ExternalOutput").ap()

    def din(name, shape):
        if name in ctx.din:
            return
        ctx.din[name] = nc.dram_tensor(name, list(shape), F32, kind="ExternalInput").ap()

    for n, sh in [("C_ONESB", (128, 1)), ("C_ONESF1", (1, 128)),
                  ("C_EPS", (1, 1)), ("C_NEGH", (1, 1))]:
        din(n, sh)
    if "l0_ffn" in need:
        din("RMT", (16, 4096))
        din("RUT", (16, 16384))
        din("WCT", (16384, 1024))
        ctx.dram["CDRAM"] = nc.dram_tensor("CDRAM", [4 * 64 * 16 * 64], BF16,
                                           kind="Internal").ap()
        ctx.dram["CBLKD"] = nc.dram_tensor("CBLKD", [128 * 4096], BF16,
                                           kind="Internal").ap()
    if "l0_attn" in need:
        for nme in ["L0WQT", "L0WVT", "L0WGT", "L0WOT"]:
            din(nme, (1024, 1024))
        din("B128", (128, 8))
        din("L0G", (128, 8))
        din("L0B", (128, 8))
        din("C_NEGLN2", (128, 1))
    if "l1_ffn" in need:
        din("L1GWT", (128, 64))
        din("L1GBr", (1, 8))
        din("L1EBL", (128, 64))
        din("L1EW", (8 * 1024, 1024))
    if "l2f_dbg" in need:
        need = set(list(need) + ["l2_ffn"])
    if "l2_ffn" in need:
        for l_ in range(2):
            din("L2QKV%d" % l_, (1024, 1536))
            din("L2WO%d" % l_, (512, 1024))
            din("L2W1P%d" % l_, (1024, 5460))
            din("L2W2%d" % l_, (2730, 1024))
        din("L2B1A", (128, 44))
        din("L2B1G", (128, 44))
        din("L2B2", (128, 16))
        din("L2GL", (128, 16))
        din("L2BL", (128, 16))
        din("L2QSL", (128, 2))
        din("L2KSL", (128, 2))
        din("C_ONEB2", (128, 128))
        din("C_MKA", (128, 4096))
        din("C_CSQ", (256, 2048))
        din("C_CSK", (256, 2048))
        din("C_NEG20", (128, 1))
        din("C_IDENF", (128, 128))
        ctx.dram["RXD"] = nc.dram_tensor("RXD", [1024 * 2048], F32, kind="Internal").ap()
    if "l2_attn" in need:
        din("WQKV2Q", (1024, 1024))
        din("WQKV2K", (1024, 1024))
        din("WQKV2V", (1024, 1024))
        din("WPRJ2", (1024, 1024))
        din("QB2L", (128, 8))
        din("KB2r", (1, 1024))
        din("VB2r", (1, 1024))
        din("PB2L", (128, 8))
    if "l3_attn" in need:
        din("QKV3T", (1024, 1536))
        din("OW3T", (1024, 1024))
        din("QB3L", (128, 10))
        din("VB3r", (1, 256))
        din("OB3L", (128, 8))
        din("L3SCL", (64, 1))
        din("C_NEG20", (128, 1))
        din("C_IDENF", (128, 128))
    if "l3_ffn" in need:
        din("GW3L", (128, 64))
        din("W03T", (1024, 1024))
        din("W13T", (1024, 1024))
        din("B03L", (128, 8))
        din("B13L", (128, 8))
        din("C_IDENF", (128, 128))
    if "l1_attn" in need:
        for nme in ["L1WQ", "L1WK", "L1WV"]:
            din(nme, (1024, 1024))
        din("L1BQL", (128, 8))
        din("L1BKr", (1, 1024))
        din("L1BVr", (1, 1024))
        din("L1TROW", (1, 16))
        din("C_NEG20", (128, 1))
        din("C_IDENF", (128, 128))
        ctx.dram["L1KD"] = nc.dram_tensor("L1KD", [16 * 128 * 1024], BF16, kind="Internal").ap()
        ctx.dram["L1VD"] = nc.dram_tensor("L1VD", [16 * 128 * 1024], BF16, kind="Internal").ap()
        ctx.dram["L1PD"] = nc.dram_tensor("L1PD", [2048 * 1024], BF16, kind="Internal").ap()

    with ExitStack() as stack:
        ctx.alloc(stack, "RX", [128, NJ * T], F32)
        ctx.alloc(stack, "XN", [128, NJ * T], BF16)
        ctx.alloc(stack, "POOL", [128, NJ], F32)
        ctx.alloc(stack, "ONESB", [128, 1], BF16)
        ctx.alloc(stack, "ONESF1", [1, 128], F32)
        ctx.alloc(stack, "EPS", [1, 1], F32)
        ctx.alloc(stack, "NEGH", [1, 1], F32)
        ctx.ps["PA"] = stack.enter_context(nc.psum_tensor("PA", [128, 4 * TC], F32))
        ctx.ps["PB"] = stack.enter_context(nc.psum_tensor("PB", [128, 3 * TC], F32))
        ctx.ps["PBT"] = stack.enter_context(nc.psum_tensor("PBT", [128, 1024], BF16))
        for e in Prog.SEMS:
            pr.sem[e] = stack.enter_context(nc.semaphore("sem_" + e))
        blk = stack.enter_context(nc.Block())

        RX = ctx.sb["RX"]
        pr.op("y", lambda y: y.dma_start(
            out=RX[:, :].rearrange("p (j t) -> p j t", j=NJ),
            in_=x_t.rearrange("(j p) t -> p j t", p=128)), inc=16)
        pr.op("y", lambda y: y.dma_start(out=ctx.sb["ONESF1"][:, :],
                                         in_=ctx.din["C_ONESF1"][:, :]), inc=16)
        pr.op("y", lambda y: y.dma_start(out=ctx.sb["EPS"][:, :],
                                         in_=ctx.din["C_EPS"][:, :]), inc=16)
        pr.op("y", lambda y: y.dma_start(out=ctx.sb["NEGH"][:, :],
                                         in_=ctx.din["C_NEGH"][:, :]), inc=16)
        pr.op("g", lambda g: g.dma_start(out=ctx.sb["ONESB"][:, :],
                                         in_=ctx.din["C_ONESB"][:, :]), inc=16)
        if "l1_ffn" in need:
            for dst, srcn in [("GWT", "L1GWT"), ("L1EB", "L1EBL")]:
                pr.op("y", lambda y, dst=dst, srcn=srcn: y.dma_start(
                    out=ctx.sb[dst][:, :], in_=ctx.din[srcn][:, :]), inc=16)
        if "l1_attn" in need:
            for dst, srcn in [("L1BQ", "L1BQL"), ("NEG20", "C_NEG20")]:
                pr.op("y", lambda y, dst=dst, srcn=srcn: y.dma_start(
                    out=ctx.sb[dst][:, :], in_=ctx.din[srcn][:, :]), inc=16)
            pr.op("g", lambda g: g.dma_start(out=ctx.sb["IDENB"][:, :],
                                             in_=ctx.din["C_IDENF"][:, :]), inc=16)
        if "l2_ffn" in need:
            for dst, srcn in [("L2B1A", "L2B1A"), ("L2B1G", "L2B1G"),
                              ("L2B2", "L2B2"), ("L2GB", "L2GL"), ("L2BB", "L2BL"),
                              ("L2QS", "L2QSL"), ("L2KS", "L2KSL"),
                              ("NEG20", "C_NEG20")]:
                pr.op("y", lambda y, dst=dst, srcn=srcn: y.dma_start(
                    out=ctx.sb[dst][:, :], in_=ctx.din[srcn][:, :]), inc=16)
            pr.op("g", lambda g: g.dma_start(out=ctx.sb["ONEB2"][:, :],
                                             in_=ctx.din["C_ONEB2"][:, :]), inc=16)
            pr.op("g", lambda g: g.dma_start(out=ctx.sb["IDENB"][:, :],
                                             in_=ctx.din["C_IDENF"][:, :]), inc=16)
        if "l2_attn" in need:
            for dst, srcn in [("QB2", "QB2L"), ("PB2", "PB2L")]:
                pr.op("y", lambda y, dst=dst, srcn=srcn: y.dma_start(
                    out=ctx.sb[dst][:, :], in_=ctx.din[srcn][:, :]), inc=16)
        if "l3_attn" in need:
            for dst, srcn in [("QB3", "QB3L"), ("OB3", "OB3L"), ("NEG20", "C_NEG20")]:
                pr.op("y", lambda y, dst=dst, srcn=srcn: y.dma_start(
                    out=ctx.sb[dst][:, :], in_=ctx.din[srcn][:, :]), inc=16)
            pr.op("y", lambda y: y.dma_start(out=ctx.sb["SCL64"][0:64, 0:1],
                                             in_=ctx.din["L3SCL"][:, :]), inc=16)
            pr.op("v", lambda v: v.memset(ctx.sb["NEGONE"][:, :], -1.0), inc=1)
            pr.op("v", lambda v: v.memset(ctx.sb["ONEC"][:, :], 1.0), inc=1)
            pr.op("v", lambda v: v.drain(), raw=True)
        if "l3_ffn" in need:
            for dst, srcn in [("B03", "B03L"), ("B13", "B13L")]:
                ctx.alloc(stack, dst, [128, 8], F32)
                pr.op("y", lambda y, dst=dst, srcn=srcn: y.dma_start(
                    out=ctx.sb[dst][:, :], in_=ctx.din[srcn][:, :]), inc=16)
            pr.op("y", lambda y: y.dma_start(out=ctx.sb["IDENFF"][:, :],
                                             in_=ctx.din["C_IDENF"][:, :]), inc=16)
        if "l0_attn" in need:
            pr.op("y", lambda y: y.dma_start(out=ctx.sb["B128s"][:, :],
                                             in_=ctx.din["B128"][:, :]), inc=16)
            pr.op("y", lambda y: y.dma_start(out=ctx.sb["L0G"][:, :],
                                             in_=ctx.din["L0G"][:, :]), inc=16)
            pr.op("y", lambda y: y.dma_start(out=ctx.sb["L0B"][:, :],
                                             in_=ctx.din["L0B"][:, :]), inc=16)
            pr.op("y", lambda y: y.dma_start(out=ctx.sb["NEGLN2"][:, :],
                                             in_=ctx.din["C_NEGLN2"][:, :]), inc=16)
        pr.barrier()

        ARA = ctx.alloc(stack, "ARA", [128, 16384], BF16)
        ARB = ctx.alloc(stack, "ARB", [128, 16384], BF16)
        ctx.sb["WBUF"] = ARA
        ctx.sb["HBLK"] = ARB[:, 0:16 * TC]
        ctx.sb["CBLK"] = ARB[:, 16 * TC:16 * TC + 4096]
        ctx.alloc(stack, "STATR", [1, 3 * T], F32)
        ctx.alloc(stack, "RSTB", [128, T], BF16)
        SCR2 = ctx.alloc(stack, "SCR2", [128, T], BF16)
        ctx.sb["MRB"] = SCR2[:, 0:T]
        ctx.sb["GSC"] = SCR2[:, 0:1024]
        ctx.sb["BCB"] = SCR2[:, 1024:2048]
        CTS = ctx.alloc(stack, "CTS", [128, 2 * T], BF16)
        ctx.sb["CT0"] = CTS[:, 0:T]
        ctx.sb["CT1"] = CTS[:, T:2 * T]
        ctx.sb["KVST"] = CTS[:, 0:T]
        ctx.alloc(stack, "SMALLS", [128, 32], F32)
        if "l0_attn" in need:
            for nme, sh in [("B128s", (128, 8)), ("L0G", (128, 8)),
                            ("L0B", (128, 8)), ("NEGLN2", (128, 1))]:
                ctx.alloc(stack, nme, list(sh), F32)
        SRH = ctx.sb["STATR"]
        ctx.sb["L1BKr"] = SRH[0:1, 0:1024]
        ctx.sb["L1BVr"] = SRH[0:1, 1024:2048]
        ctx.sb["TROW"] = SRH[0:1, 2048:2064]
        ctx.sb["LGROW"] = SRH[0:1, 2064:2128]
        ctx.sb["NEGB8"] = SRH[0:1, 2128:2136]
        ctx.sb["L1GB"] = SRH[0:1, 2136:2144]
        if "l1_ffn" in need:
            for nme, sh, dt_ in [("GWT", (128, 64), F32), ("U8S", (1, 64), U8),
                                 ("GAT128", (128, 8), F32),
                                 ("L1EB", (128, 64), F32)]:
                ctx.alloc(stack, nme, list(sh), dt_)
        if "l2_attn" in need:
            for nme in ["QB2", "PB2"]:
                ctx.alloc(stack, nme, [128, 8], F32)

        if "l2_ffn" in need:
            for nme, sh, dt_ in [("L2B1A", (128, 44), F32), ("L2B1G", (128, 44), F32),
                                 ("L2B2", (128, 16), F32), ("L2GB", (128, 16), F32),
                                 ("L2BB", (128, 16), F32), ("L2QS", (128, 2), F32),
                                 ("L2KS", (128, 2), F32), ("ONEB2", (128, 128), BF16)]:
                ctx.alloc(stack, nme, list(sh), dt_)
            ctx.sb["L2G0"] = ctx.sb["L2GB"][:, 0:8]
            ctx.sb["L2G1"] = ctx.sb["L2GB"][:, 8:16]
            ctx.sb["L2B0"] = ctx.sb["L2BB"][:, 0:8]
            ctx.sb["L2B1"] = ctx.sb["L2BB"][:, 8:16]
            if "NEG20" not in ctx.sb:
                ctx.alloc(stack, "NEG20", [128, 1], F32)
            if "IDENB" not in ctx.sb:
                ctx.alloc(stack, "IDENB", [128, 128], BF16)
        if "l3_attn" in need:
            for nme, sh, dt_ in [("QB3", (128, 10), F32), ("OB3", (128, 8), F32),
                                 ("SCL64", (128, 2), F32), ("NEGONE", (128, 1), F32),
                                 ("ONEC", (128, 1), F32)]:
                ctx.alloc(stack, nme, list(sh), dt_)
        if "l3_ffn" in need:
            for nme, sh, dt_ in [("LG8", (128, 48), F32), ("U8T", (128, 8), U8),
                                 ("NEGT", (128, 8), F32), ("S0C", (128, 16), F32),
                                 ("S1C", (128, 16), F32), ("ST16", (16, 128), F32)]:
                ctx.alloc(stack, nme, list(sh), dt_)
        if ("l3_attn" in need or "l3_ffn" in need):
            ctx.alloc(stack, "IDENFF", [128, 128], F32)
        if "l3_attn" in need and "NEG20" not in ctx.sb:
            ctx.alloc(stack, "NEG20", [128, 1], F32)

        ctx.sb["VB3r"] = ctx.sb["STATR"][0:1, 2144:2400]
        if "l1_attn" in need:
            for nme, sh, dt_ in [("L1BQ", (128, 8), F32),
                                 ("TMP128", (128, 16), F32), ("ZROW", (128, 8), F32),
                                 ("QLB", (128, 128), BF16), ("NEG20", (128, 1), F32),
                                 ("IDENB", (128, 128), BF16)]:
                if nme not in ctx.sb:
                    ctx.alloc(stack, nme, list(sh), dt_)
        if "l0_ffn" in need:
            emit_l0_ffn_prep(ctx, stack)

        for tag in sublayers:
            if tag == "l0_ffn":
                emit_glue_ln(ctx)
                pr.barrier()
                emit_l0_ffn(ctx, coef=0.5)
                pr.barrier()
            elif tag.startswith("ln_dbg"):
                sa = int(tag[6:]) if len(tag) > 6 else 99
                emit_glue_ln(ctx, stop_after=sa)
                pr.barrier()
                if sa <= 3:
                    for r in range(4):
                        pr.op("v", lambda v, r=r: v.tensor_copy(
                            ctx.sb["RX"][0:1, r * T:(r + 1) * T],
                            ctx.sb["STATR"][0:1, r * T:(r + 1) * T]), inc=1)
                else:
                    for j in range(NJ):
                        pr.op("v", lambda v, j=j: v.tensor_copy(
                            ctx.sb["RX"][:, j * T:(j + 1) * T],
                            ctx.sb["XN"][:, j * T:(j + 1) * T]), inc=1)
                pr.barrier()
            elif tag == "l0_attn":
                emit_glue_ln(ctx)
                pr.barrier()
                emit_l0_attn(ctx)
                pr.barrier()
            elif tag == "l1_ffn":
                emit_glue_ln(ctx, want_pooled=True)
                pr.barrier()
                emit_l1_ffn(ctx, coef=0.5)
                pr.barrier()
            elif tag == "l1_attn":
                emit_glue_ln(ctx)
                pr.barrier()
                emit_l1_attn(ctx)
                pr.barrier()
            elif tag == "l2f_dbg":
                emit_glue_ln(ctx)
                pr.barrier()
                emit_l2_ffn(ctx, coef=0.5, dbg_stop=DBG_STOP)
                pr.barrier()
                if DBG_STOP in ("proj", "norm", "rotary"):
                    for j in range(NJ):
                        pr.op("v", lambda v, j=j: v.tensor_copy(
                            ctx.sb["RX"][:, j * T:(j + 1) * T],
                            ctx.sb["ARA"][:, j * T:(j + 1) * T]), inc=1)
                pr.barrier()
            elif tag == "l2_ffn":
                emit_glue_ln(ctx)
                pr.barrier()
                emit_l2_ffn(ctx, coef=0.5)
                pr.barrier()
            elif tag == "l2_attn":
                emit_glue_ln(ctx)
                pr.barrier()
                emit_l2_attn(ctx)
                pr.barrier()
            elif tag == "l3_attn":
                emit_glue_ln(ctx)
                pr.barrier()
                emit_l3_attn(ctx)
                pr.barrier()
            elif tag == "l3_ffn":
                emit_glue_ln(ctx)
                pr.barrier()
                emit_l3_ffn(ctx, coef=0.5)
                pr.barrier()
            elif tag == "noop":
                pr.barrier()
            else:
                raise NotImplementedError(tag)

        pr.op("y", lambda y: y.dma_start(
            out=y_t.rearrange("(j p) t -> p j t", p=128),
            in_=RX[:, :].rearrange("p (j t) -> p j t", j=NJ)), inc=16)
        pr.wait("y", "dy")
        pr.replay(blk)
    return nc, list(ctx.din.keys())


def kernel(**inputs):
    x = np.asarray(inputs["L0_attn_x"], np.float32)
    lay = lay_inputs(inputs)
    consts = make_consts()
    nc, names = build_program(
        ["l0_ffn", "l0_attn", "l0_ffn", "l1_ffn", "l1_attn", "l1_ffn",
         "l2_ffn", "l2_attn", "l2_ffn", "l3_ffn", "l3_attn", "l3_ffn"])
    in_maps = []
    for core in range(8):
        m = {"x_t": np.ascontiguousarray(x[core].T)}
        for k in names:
            m[k] = consts[k] if k.startswith("C_") else lay[k]
        in_maps.append(m)
    res = run_bass_kernel_spmd(nc, in_maps, core_ids=list(range(8)))
    return np.stack([np.ascontiguousarray(res.results[i]["y_t"].T)
                     for i in range(8)], axis=0)
